# version 20
# baseline (speedup 1.0000x reference)
import numpy as np
from contextlib import ExitStack

import concourse.bass as bass
import concourse.mybir as mybir
from concourse.bass_utils import run_bass_kernel_spmd

F32 = mybir.dt.float32
F32R = mybir.dt.float32r
BF16 = mybir.dt.bfloat16
U32 = mybir.dt.uint32
AF = mybir.ActivationFunctionType
ALU = mybir.AluOpType
AX = mybir.AxisListType

T, D, NH, DH, FF = 2048, 1024, 8, 128, 4096
NCH, NG, DC, FC = 16, 4, 8, 32
SCALE = 1.0 / float(np.sqrt(DH))

EMAP = {"PE": "tensor", "DVE": "vector", "ACT": "scalar", "POOL": "gpsimd",
        "SYNCQ": "sync", "ACTQ": "scalar", "POOLQ": "gpsimd"}
ENGS = ("PE", "DVE", "ACT", "POOL")


class TR:

    def __init__(self):
        self.cnt = {}
        self.lastw = {}
        self.readers = {}
        self.waited = {}
        self.floor = {}
        self.ops = {k: [] for k in ("tensor", "vector", "scalar", "gpsimd", "sync")}
        self.sem_names = set(ENGS)

    def barrier(self):
        self.floor = dict(self.cnt)

    def _wl(self, stream, r, w):
        need = dict(self.floor)
        for k in r:
            lw = self.lastw.get(k)
            if lw:
                need[lw[0]] = max(need.get(lw[0], 0), lw[1])
        for k in w:
            lw = self.lastw.get(k)
            if lw:
                need[lw[0]] = max(need.get(lw[0], 0), lw[1])
            for s2, v2 in self.readers.get(k, {}).items():
                need[s2] = max(need.get(s2, 0), v2)
        out = []
        wt = self.waited.setdefault(stream, {})
        for s2 in sorted(need):
            if wt.get(s2, 0) < need[s2]:
                wt[s2] = need[s2]
                out.append((s2, need[s2]))
        return out

    def _commit(self, sem, val, r, w):
        for k in w:
            self.lastw[k] = (sem, val)
            self.readers[k] = {}
        for k in r:
            self.readers.setdefault(k, {})[sem] = val

    def op(self, eng, fn, r=(), w=()):
        wl = self._wl(eng, r, w)
        self.cnt[eng] = self.cnt.get(eng, 0) + 1
        val = self.cnt[eng]

        def emit(engobj, sems, _wl=wl, _fn=fn, _eng=eng):
            for s2, v2 in _wl:
                engobj.wait_ge(sems[s2], v2)
            _fn(engobj).then_inc(sems[_eng], 1)
        self.ops[EMAP[eng]].append(emit)
        self._commit(eng, val, r, w)

    def dma(self, q, dsem, fn, r=(), w=()):
        self.sem_names.add(dsem)
        wl = self._wl(q, r, w)
        self.cnt[dsem] = self.cnt.get(dsem, 0) + 16
        val = self.cnt[dsem]

        def emit(engobj, sems, _wl=wl, _fn=fn, _ds=dsem):
            for s2, v2 in _wl:
                engobj.wait_ge(sems[s2], v2)
            _fn(engobj).then_inc(sems[_ds], 16)
        self.ops[EMAP[q]].append(emit)
        self._commit(dsem, val, r, w)

    def final_wait(self, stream, sem):
        v = self.cnt.get(sem, 0)
        self.ops[EMAP[stream]].append(
            lambda e, s, _sm=sem, _v=v: e.wait_ge(s[_sm], _v) if _v else None)

    def build_blocks(self, nc, ctx):
        sems = {n: ctx.enter_context(nc.semaphore(f"s_{n}"))
                for n in sorted(self.sem_names)}
        with nc.Block() as block:
            @block.sync
            def _(eng):
                for emit in self.ops["sync"]:
                    emit(eng, sems)

            @block.tensor
            def _(eng):
                for emit in self.ops["tensor"]:
                    emit(eng, sems)

            @block.vector
            def _(eng):
                for emit in self.ops["vector"]:
                    emit(eng, sems)

            @block.scalar
            def _(eng):
                for emit in self.ops["scalar"]:
                    emit(eng, sems)

            @block.gpsimd
            def _(eng):
                for emit in self.ops["gpsimd"]:
                    emit(eng, sems)


def build(upto=9):
    import ml_dtypes
    nc = bass.Bass("TRN2", target_bir_lowering=False, debug=False)

    def din(name, shape, dt=F32):
        return nc.dram_tensor(name, shape, dt, kind="ExternalInput").ap()
    xT = din("xT", [D, T], F32R)
    xTf = din("xTf", [D, T])
    wqT = din("wqT", [D, D]); wkT = din("wkT", [D, D]); wvT = din("wvT", [D, D])
    woutT = din("woutT", [D, D])
    wgP = din("wgP", [D, 128])
    b128 = din("b128", [128, 1])
    nw8 = din("nw8", [128, DC])
    gw8T = din("gw8T", [D, 8])
    acatT = din("acatT", [D, 32])
    g8 = din("g8", [8, 1])
    bcatT = din("bcatT", [32, FF])
    wiT = din("wiT", [D, FF])
    woT = din("woT", [FF, D])
    wib32 = din("wib32", [128, FC])
    wob8 = din("wob8", [128, DC])
    bout8 = din("bout8", [128, DC])
    outT = nc.dram_tensor("outT", [D, T], F32, kind="ExternalOutput").ap()
    x1h = nc.dram_tensor("x1h", [D, T], F32, kind="Internal").ap()

    eye_f = nc.inline_tensor(np.eye(128, dtype=np.float32), "eye_f").ap()
    eye_b = nc.inline_tensor(np.eye(128, dtype=ml_dtypes.bfloat16), "eye_b").ap()
    ones_f = nc.inline_tensor(np.ones((128, 128), np.float32), "ones_f").ap()
    m01 = (np.arange(128)[:, None] <= np.arange(128)[None, :]).astype(np.float64)
    mask8_np = np.tile(m01, (1, 8)).astype(ml_dtypes.bfloat16)
    mask8_c = nc.inline_tensor(mask8_np, "mask8").ap()
    hm_np = np.zeros((8, 1024), np.float64)
    for h in range(8):
        hm_np[h, h * 128:(h + 1) * 128] = 1.0
    hmask_c = nc.inline_tensor(hm_np.astype(ml_dtypes.bfloat16), "hmask8").ap()
    iota8_c = nc.inline_tensor(
        np.broadcast_to(np.arange(8, dtype=np.float32), (128, 8)).copy(), "iota8").ap()
    exp_np = np.zeros((8, 32), ml_dtypes.bfloat16)
    for e in range(8):
        exp_np[e, e * 4:(e + 1) * 4] = 1.0
    expand_c = nc.inline_tensor(exp_np, "expand832").ap()
    rot = np.zeros((128, 128), np.float64)
    rot[np.arange(64), 64 + np.arange(64)] = -1.0
    rot[64 + np.arange(64), np.arange(64)] = 1.0
    rotT_c = nc.inline_tensor(rot.T.astype(ml_dtypes.bfloat16), "rotT").ap()
    sel = np.zeros((128, 16), np.float32)
    sel[64 + np.arange(8), np.arange(8)] = 1.0
    sel[32 + np.arange(8), 8 + np.arange(8)] = 1.0
    sel_c = nc.inline_tensor(sel, "sel16").ap()
    half = DH // 2
    inv_freq = 1.0 / (10000.0 ** (np.arange(half, dtype=np.float64) / half))
    frq = np.arange(T, dtype=np.float64)[None, :] * inv_freq[:, None]
    cs2 = np.concatenate([np.cos(frq), np.cos(frq)], axis=0).astype(ml_dtypes.bfloat16)
    sn2 = np.concatenate([np.sin(frq), np.sin(frq)], axis=0).astype(ml_dtypes.bfloat16)
    ctab_c = nc.inline_tensor(cs2, "ctab").ap()
    stab_c = nc.inline_tensor(sn2, "stab").ap()
    z8_c = nc.inline_tensor(np.zeros((8, 128), np.float32), "z8").ap()

    tr = TR()

    class _Done(Exception):
        pass

    def ldc(dst, src, key):
        tr.dma("SYNCQ", "dc_" + key.replace(":", ""),
               lambda e, _d=dst, _s=src: e.dma_start(out=_d, in_=_s), w=[key])

    def pe_settle(ps_ap, keys):
        tr.op("PE", lambda e, _p=ps_ap: e.matmul(
            _p, lhsT=onesf_s[0:1, 0:2], rhs=onesf_s[0:1, 0:2],
            start=True, stop=True), r=["c:onesf"], w=list(keys))

    ctx = ExitStack()
    try:
      with ctx:
        SB = lambda n, s, d=F32: ctx.enter_context(nc.sbuf_tensor(n, s, d))
        eyef_s = SB("eyef", [128, 128])
        eyeb_s = SB("eyeb", [128, 128], BF16)
        onesf_s = SB("onesf", [128, 128]); onesr_s = SB("onesr", [128, 128], F32R)
        onesb_s = SB("onesb", [128, 128], BF16)
        iota8_s = SB("iota8s", [128, 8])
        expand_s = SB("expand", [8, 32], BF16)
        nw_s = SB("nw", [128, DC]); b128_s = SB("b128s", [128, 1])
        g8_s = SB("g8s", [8, 1])
        wib_s = SB("wibc", [128, FC]); wob_s = SB("wobc", [128, DC]); bout_s = SB("boutc", [128, DC])
        cb_s = SB("cbias", [128, 4])
        ldc(eyef_s[:, :], eye_f, "c:eyef")
        ldc(eyeb_s[:, :], eye_b, "c:eyeb")
        ldc(onesf_s[:, :], ones_f, "c:onesf")
        ldc(iota8_s[:, :], iota8_c, "c:iota"); ldc(expand_s[:, :], expand_c, "c:exp")
        ldc(nw_s[:, :], nw8, "c:nw"); ldc(b128_s[:, :], b128, "c:b128")
        ldc(g8_s[:, :], g8, "c:g8"); ldc(wib_s[:, :], wib32, "c:wib")
        ldc(wob_s[:, :], wob8, "c:wob"); ldc(bout_s[:, :], bout8, "c:bout")
        for j, v in ((0, 1e-11), (1, 1.0), (2, 1e-5),
                     (3, float(np.log(SCALE)))):
            tr.op("DVE", lambda e, _j=j, _v=v: e.memset(cb_s[:, _j:_j + 1], _v), w=["c:cb"])
        tr.op("DVE", lambda e: e.tensor_copy(out=onesr_s[:, :], in_=onesf_s[:, :]),
              r=["c:onesf"], w=["c:onesr"])
        tr.op("DVE", lambda e: e.tensor_copy(out=onesb_s[:, :], in_=onesf_s[:, :]),
              r=["c:onesf"], w=["c:onesb"])

        with ExitStack() as attn:
            _a = attn.enter_context
            ct_s = [_a(nc.sbuf_tensor(f"cts{i}", [128, 512], BF16)) for i in range(2)]
            st_s = [_a(nc.sbuf_tensor(f"sts{i}", [128, 512], BF16)) for i in range(2)]
            rotT_s = _a(nc.sbuf_tensor("rotTs", [128, 128], BF16))
            mask8_s = _a(nc.sbuf_tensor("mask8s", [128, 1024], BF16))
            hmask_s = _a(nc.sbuf_tensor("hmasks", [8, 1024], BF16))
            sel_s = _a(nc.sbuf_tensor("sel16s", [128, 16], F32))
            dg8_s = _a(nc.sbuf_tensor("dg8s", [8, 32], F32))
            z8_s = _a(nc.sbuf_tensor("z8s", [8, 128], F32))
            xsl3 = [_a(nc.sbuf_tensor(f"xsl3{i}", [128, DC, 256], F32R)) for i in range(2)]
            xsq = _a(nc.sbuf_tensor("xsq", [128, DC, 256], BF16))
            xhat = [_a(nc.sbuf_tensor(f"xhat{i}", [128, DC, 512], BF16)) for i in range(2)]
            rrep_s = _a(nc.sbuf_tensor("rrep", [128, 512], F32))
            t1_s = _a(nc.sbuf_tensor("t1s", [128, 256], F32))
            mrow_s = _a(nc.sbuf_tensor("mrow", [1, 1536], F32))
            mrr_s = _a(nc.sbuf_tensor("mrr", [1, 1024], F32R))
            wg_s = _a(nc.sbuf_tensor("wg", [128, DC, 128], F32R))
            wsg_s = _a(nc.sbuf_tensor("wsg", [1, 128], F32R))
            colt_s = _a(nc.sbuf_tensor("colt", [128, 1], F32))
            zt_s = _a(nc.sbuf_tensor("zt", [128, 256], F32))
            sg1_s = _a(nc.sbuf_tensor("sg1", [128, 256], F32))
            ls_s = _a(nc.sbuf_tensor("ls", [128, 512], F32))
            lhat_s = _a(nc.sbuf_tensor("lhat", [8, 512], F32))
            lgml_s = _a(nc.sbuf_tensor("lgml", [8, 512], F32))
            lbuf_s = _a(nc.sbuf_tensor("lbuf", [8, 512], F32))
            ehat_s = [_a(nc.sbuf_tensor(f"ehat{i}", [8, 512], F32)) for i in range(2)]
            em8_s = _a(nc.sbuf_tensor("em8", [8, 1024], BF16))
            tmg_s = [_a(nc.sbuf_tensor(f"tmg{i}", [128, 64], F32)) for i in range(2)]
            egml_s = _a(nc.sbuf_tensor("egml", [128, 8], F32))
            ue8_s = _a(nc.sbuf_tensor("ue8", [128, 8], F32))
            ac8_s = _a(nc.sbuf_tensor("ac8", [128, 8], F32))
            wpb = [_a(nc.sbuf_tensor(f"wpb{i}", [128, 4, 1024], BF16)) for i in range(3)]
            qb = _a(nc.sbuf_tensor("qb", [128, NH, 1024], BF16))
            kb = _a(nc.sbuf_tensor("kb", [128, NH, 1024], BF16))
            vb = _a(nc.sbuf_tensor("vb", [128, 8, D], BF16))
            ropeA = [_a(nc.sbuf_tensor(f"ropeA{i}", [128, 512], BF16)) for i in range(2)]
            tmp1 = [_a(nc.sbuf_tensor(f"tmp1{i}", [128, 512], BF16)) for i in range(2)]
            tmp2 = [_a(nc.sbuf_tensor(f"tmp2{i}", [128, 512], BF16)) for i in range(2)]
            qtl = [_a(nc.sbuf_tensor(f"qtl{i}", [128, NH, 128], BF16)) for i in range(2)]
            pbt_s = _a(nc.sbuf_tensor("pbt", [128, NH, 128], BF16))
            vtl_s = _a(nc.sbuf_tensor("vtl", [128, NH, 128], BF16))
            ktm_s = _a(nc.sbuf_tensor("ktm", [128, NH, 128], BF16))
            msb_s = _a(nc.sbuf_tensor("msb", [128, NH, DH], F32))
            mbb_s = _a(nc.sbuf_tensor("mbb", [128, NH, DH], BF16))
            yring = _a(nc.sbuf_tensor("yring", [128, NH, 512], BF16))
            aosl = _a(nc.sbuf_tensor("aosl", [128, DC, 256], F32))
            woutb = _a(nc.sbuf_tensor("woutb", [128, NH, D], BF16))
            psP0 = _a(nc.psum_tensor("psP0", [128, 512], F32))
            psP1 = _a(nc.psum_tensor("psP1", [128, 512], F32))
            psStat = _a(nc.psum_tensor("psStat", [128, 512], F32))
            psE = _a(nc.psum_tensor("psE", [128, NH * 128], F32))
            psA = _a(nc.psum_tensor("psA", [128, NH * 128], F32))
            psXK = _a(nc.psum_tensor("psXK", [128, NH * 128], BF16))
            if upto < 1:
                raise _Done()

            ldc(rotT_s[:, :], rotT_c, "c:rot"); ldc(mask8_s[:, :], mask8_c, "c:mask8")
            ldc(hmask_s[:, :], hmask_c, "c:hmask"); ldc(sel_s[:, :], sel_c, "c:sel")
            ldc(z8_s[:, :], z8_c, "c:z8")

            tr.dma("POOLQ", "d_wg",
                   lambda e: e.dma_start(out=wg_s[:, :, :],
                                         in_=wgP.rearrange("(c p) e -> p c e", p=128)),
                   w=["wg"])

            def _fold(e):
                for dc in range(DC):
                    i = e.mul(out=wg_s[:, dc, :], in_=wg_s[:, dc, :],
                              mul=nw_s[:, dc:dc + 1])
                return i
            tr.op("ACT", _fold, r=["wg", "c:nw"], w=["wg"])

            def _wgcs(e):
                for dc in range(DC):
                    i = e.matmul(psE[:, 0:2], lhsT=wg_s[:, dc, :], rhs=onesr_s[:, 0:2],
                                 start=(dc == 0), stop=(dc == DC - 1))
                return i
            tr.op("PE", _wgcs, r=["wg", "c:onesr", "psE"], w=["psE"])
            tr.op("ACT", lambda e: e.activation(out=colt_s[:, :], in_=psE[:, 0:1],
                                                func=AF.Copy),
                  r=["psE"], w=["colt"])
            tr.op("PE", lambda e: e.transpose(psE[0:1, 0:128], colt_s[:, :], eyef_s[:, :]),
                  r=["colt", "c:eyef", "psE"], w=["psE"])
            pe_settle(psE[32:34, 200:202], ["psE"])
            tr.op("ACT", lambda e: e.activation(out=wsg_s[0:1, :], in_=psE[0:1, 0:128],
                                                func=AF.Copy, scale=-1.0),
                  r=["psE"], w=["wsg"])
            tr.dma("POOLQ", "d_wout",
                   lambda e: e.dma_start(out=woutb[:, :, :],
                                         in_=woutT.rearrange("(c p) e -> p c e", p=128)),
                   w=["woutb"])

            MR = {"mrow": 0, "msq": 512, "vrow": 1024}
            RR = {"mrowR": 0, "rrow": 512}

            def mrs(k):
                return mrow_s[0:1, MR[k]:MR[k] + 512]

            def rrs(k):
                return mrr_s[0:1, RR[k]:RR[k] + 512]

            def frontS(g):
                gp = g % 2
                gsl = slice(g * 512, (g + 1) * 512)
                tr.dma("SYNCQ", f"d_ct{gp}",
                       lambda e, _gp=gp, _gsl=gsl: e.dma_start(
                           out=ct_s[_gp][:, :], in_=ctab_c[:, _gsl]),
                       w=[f"ct{gp}"])
                tr.dma("SYNCQ", f"d_st{gp}",
                       lambda e, _gp=gp, _gsl=gsl: e.dma_start(
                           out=st_s[_gp][:, :], in_=stab_c[:, _gsl]),
                       w=[f"st{gp}"])
                for hb in range(2):
                    xg256 = slice(g * 512 + hb * 256, g * 512 + (hb + 1) * 256)
                    tr.dma("SYNCQ", f"d_xsl3{hb}",
                           lambda e, _hb=hb, _s=xg256: e.dma_start(
                               out=xsl3[_hb][:, :, :],
                               in_=xT.rearrange("(c p) t -> p c t", p=128)[:, :, _s]),
                           w=[f"xsl3{hb}"])
                    tr.op("ACT", lambda e, _hb=hb: e.activation(
                        out=xsq[:, :, :].rearrange("p c t -> p (c t)"),
                        in_=xsl3[_hb][:, :, :].rearrange("p c t -> p (c t)"),
                        func=AF.Square),
                        r=[f"xsl3{hb}"], w=["xsq"])

                    def _sums(e, _hb=hb):
                        for dc in range(DC):
                            e.matmul(psStat[0:1, _hb * 256:(_hb + 1) * 256],
                                     lhsT=onesr_s[:, 0:1], rhs=xsl3[_hb][:, dc, :],
                                     start=(dc == 0), stop=(dc == DC - 1))
                        for dc in range(DC):
                            i = e.matmul(psP1[0:1, _hb * 256:(_hb + 1) * 256],
                                         lhsT=onesb_s[:, 0:1], rhs=xsq[:, dc, :],
                                         start=(dc == 0), stop=(dc == DC - 1))
                        return i
                    tr.op("PE", _sums,
                          r=[f"xsl3{hb}", "xsq", "c:onesr", "c:onesb", "psStat", "psP1"],
                          w=["psStat", "st:rows", "psP1"])
                tr.op("ACT", lambda e: e.activation(out=mrs("mrow"), in_=psStat[0:1, :],
                                                    func=AF.Copy, scale=1.0 / D),
                      r=["st:rows"], w=["r:mrow"])
                tr.op("ACT", lambda e: e.activation(out=rrs("mrowR"), in_=psStat[0:1, :],
                                                    func=AF.Copy, scale=1.0 / D),
                      r=["st:rows"], w=["r:mrowR"])
                tr.op("ACT", lambda e: e.activation(out=mrs("msq"), in_=mrs("mrow"),
                                                    func=AF.Square),
                      r=["r:mrow"], w=["r:msq"])
                tr.op("DVE", lambda e: e.tensor_scalar(out=mrs("vrow"), in0=psP1[0:1, :],
                                                       scalar1=1.0 / D, scalar2=None,
                                                       op0=ALU.mult),
                      r=["st:rows", "psP1"], w=["r:vrow"])
                tr.op("DVE", lambda e: e.tensor_sub(mrs("vrow"), mrs("vrow"), mrs("msq")),
                      r=["r:vrow", "r:msq"], w=["r:vrow"])
                tr.op("ACT", lambda e: e.activation(out=mrs("vrow"), in_=mrs("vrow"),
                                                    func=AF.Ln, scale=1.000001,
                                                    bias=cb_s[0:1, 0:1]),
                      r=["r:vrow", "c:cb"], w=["r:vrow"])
                tr.op("ACT", lambda e: e.activation(out=rrs("rrow"), in_=mrs("vrow"),
                                                    func=AF.Exp, scale=-0.5),
                      r=["r:vrow"], w=["r:rrow"])
                tr.op("PE", lambda e: e.matmul(psP0[:, :], lhsT=onesr_s[0:1, :],
                                               rhs=rrs("rrow"), start=True, stop=True),
                      r=["r:rrow", "c:onesr", "psP0"], w=["psP0"])
                tr.op("ACT", lambda e: e.activation(out=rrep_s[:, :], in_=psP0[:, :],
                                                    func=AF.Copy),
                      r=["psP0"], w=["rrep"])
                tr.op("PE", lambda e: e.matmul(psStat[:, :], lhsT=onesr_s[0:1, :],
                                               rhs=rrs("mrowR"), start=True, stop=True),
                      r=["r:mrowR", "c:onesr", "psStat", "st:rows"], w=["psStat", "st:mrep"])
                for hb in range(2):
                    hsl = slice(hb * 256, (hb + 1) * 256)
                    for dc in range(DC):
                        tr.op("DVE", lambda e, _hb=hb, _dc=dc, _hsl=hsl: e.tensor_sub(
                            t1_s[:, :], xsl3[_hb][:, _dc, :], psStat[:, _hsl]),
                            r=[f"xsl3{hb}", "st:mrep"], w=["t1"])
                        tr.op("DVE", lambda e, _hb=hb, _dc=dc, _hsl=hsl, _gp=gp:
                              e.scalar_tensor_tensor(
                                  out=xhat[_gp][:, _dc, _hsl], in0=t1_s[:, :],
                                  scalar=nw_s[:, _dc:_dc + 1], in1=rrep_s[:, _hsl],
                                  op0=ALU.mult, op1=ALU.mult),
                              r=["t1", "c:nw", "rrep"], w=[f"xhat{gp}.{dc}"])
            def frontG(g):
                gp = g % 2
                gsl = slice(g * 512, (g + 1) * 512)
                for hb in range(2):
                    hsl = slice(hb * 256, (hb + 1) * 256)

                    def _zg(e, _hb=hb):
                        for dc in range(DC):
                            e.matmul(psP0[:, 0:256], lhsT=wg_s[:, dc, :],
                                     rhs=xsl3[_hb][:, dc, :], start=(dc == 0), stop=False)
                        return e.matmul(psP0[:, 0:256], lhsT=wsg_s[0:1, :],
                                        rhs=mrr_s[0:1, _hb * 256:(_hb + 1) * 256],
                                        start=False, stop=True)
                    tr.op("PE", _zg, r=["wg", f"xsl3{hb}", "wsg", "r:mrowR", "psP0"],
                          w=["psP0"])
                    tr.op("DVE", lambda e, _hsl=hsl: e.tensor_mul(zt_s[:, :], psP0[:, 0:256],
                                                                  rrep_s[:, _hsl]),
                          r=["psP0", "rrep"], w=["zt"])
                    tr.op("DVE", lambda e: e.tensor_scalar(out=zt_s[:, :], in0=zt_s[:, :],
                                                           scalar1=b128_s[:, :], scalar2=None,
                                                           op0=ALU.add),
                          r=["zt", "c:b128"], w=["zt"])
                    tr.op("ACT", lambda e: e.activation(out=sg1_s[:, :], in_=zt_s[:, :],
                                                        func=AF.Abs),
                          r=["zt"], w=["sg1"])
                    tr.op("ACT", lambda e: e.activation(out=sg1_s[:, :], in_=sg1_s[:, :],
                                                        func=AF.Exp, scale=-1.0),
                          r=["sg1"], w=["sg1"])
                    tr.op("ACT", lambda e: e.activation(out=sg1_s[:, :], in_=sg1_s[:, :],
                                                        func=AF.Ln, bias=cb_s[:, 1:2]),
                          r=["sg1", "c:cb"], w=["sg1"])
                    tr.op("DVE", lambda e: e.tensor_scalar_min(zt_s[:, :], zt_s[:, :], 0.0),
                          r=["zt"], w=["zt"])
                    tr.op("DVE", lambda e, _hsl=hsl: e.tensor_sub(ls_s[:, _hsl], zt_s[:, :],
                                                                  sg1_s[:, :]),
                          r=["zt", "sg1"], w=[f"ls.{hb}"])
                    tr.op("PE", lambda e, _hsl=hsl: e.matmul(
                        psP1[0:8, 0:256], lhsT=sel_s[:, 0:8], rhs=ls_s[:, _hsl],
                        start=True, stop=True),
                        r=["c:sel", f"ls.{hb}", "psP1"], w=["psP1"])
                    tr.op("ACT", lambda e, _hsl=hsl: e.activation(out=lhat_s[:, _hsl],
                                                                  in_=psP1[0:8, 0:256],
                                                                  func=AF.Copy),
                          r=["psP1"], w=["lhat"])
                    tr.op("PE", lambda e, _hsl=hsl: e.matmul(
                        psP1[0:8, 0:256], lhsT=sel_s[:, 8:16], rhs=ls_s[:, _hsl],
                        start=True, stop=True),
                        r=["c:sel", f"ls.{hb}", "psP1"], w=["psP1"])
                    tr.op("ACT", lambda e, _hsl=hsl: e.activation(out=lgml_s[:, _hsl],
                                                                  in_=psP1[0:8, 0:256],
                                                                  func=AF.Copy),
                          r=["psP1"], w=["lgml"])
                for lc in range(4):
                    csl = slice(lc * 128, (lc + 1) * 128)
                    hb = lc // 2
                    tr.op("DVE", lambda e, _sl=csl: e.tensor_tensor_scan(
                        out=lbuf_s[:, _sl], data0=ls_s[0:8, _sl], data1=z8_s[:, :],
                        initial=0.0, op0=ALU.add, op1=ALU.add),
                        r=[f"ls.{hb}", "c:z8"], w=["lbuf"])
                    tr.op("DVE", lambda e, _sl=csl: e.tensor_add(
                        lhat_s[:, _sl], lhat_s[:, _sl], lbuf_s[:, _sl]),
                        r=["lbuf", "lhat"], w=["lhat"])
                    tr.op("DVE", lambda e, _sl=csl: e.tensor_sub(
                        lgml_s[:, _sl], lgml_s[:, _sl], lbuf_s[:, _sl]),
                        r=["lbuf", "lgml"], w=["lgml"])
                tr.op("ACT", lambda e, _gp=gp: e.activation(
                    out=ehat_s[_gp][:, :], in_=lhat_s[:, :], func=AF.Exp,
                    bias=cb_s[0:8, 3:4]),
                    r=["lhat", "c:cb"], w=[f"ehat{gp}"])
                for lc in range(4):
                    tr.op("DVE", lambda e, _lc=lc: e.tensor_scalar(
                        out=dg8_s[:, _lc * 8:(_lc + 1) * 8], in0=eyef_s[0:8, 0:8],
                        scalar1=lbuf_s[:, _lc * 128 + 127:_lc * 128 + 128], scalar2=None,
                        op0=ALU.mult),
                        r=["lbuf", "c:eyef"], w=["dg8"])

                def _hoist(e):
                    for lc in range(4):
                        e.transpose(psStat[:, lc * 16:lc * 16 + 8],
                                    lgml_s[:, lc * 128:(lc + 1) * 128], eyef_s[0:8, 0:8])
                    for lc in range(4):
                        i = e.matmul(psStat[:, lc * 16 + 8:lc * 16 + 16],
                                     lhsT=onesf_s[0:8, :],
                                     rhs=dg8_s[:, lc * 8:(lc + 1) * 8],
                                     start=True, stop=True)
                    return i
                tr.op("PE", _hoist, r=["lgml", "dg8", "c:eyef", "c:onesf"],
                      w=["psStat", "st:hoist", "st:mrep"])
                tr.op("ACT", lambda e, _gp=gp: e.activation(
                    out=tmg_s[_gp][:, :], in_=psStat[:, 0:64], func=AF.Copy),
                    r=["st:hoist"], w=[f"tmg{gp}"])

            wp_ctr = [0]

            def proj(g):
                gp = g % 2
                gsl = slice(g * 512, (g + 1) * 512)
                for pj, wsrc in (("v", wvT), ("k", wkT), ("q", wqT)):
                    sl0 = wp_ctr[0] % 3
                    sl1 = (wp_ctr[0] + 1) % 3
                    wp_ctr[0] += 2
                    for hf, wsl in ((0, sl0), (1, sl1)):
                        tr.dma("POOLQ", f"d_wp{wsl}",
                               lambda e, _w=wsrc, _hf=hf, _wsl=wsl: e.dma_start(
                                   out=wpb[_wsl][:, :, :],
                                   in_=_w.rearrange("(c p) e -> p c e", p=128)
                                   [:, _hf * 4:(_hf + 1) * 4, :]),
                               w=[f"wpb{wsl}"])
                    if pj == "v":
                        for tc4 in range(4):
                            cg = g * 4 + tc4
                            slot = cg % 8
                            for eh in range(2):
                                pp, pk = (psP0, "psP0") if eh == 0 else (psP1, "psP1")

                                def _vg(e, _tc=tc4, _eh=eh, _pp=pp, _gp=gp,
                                        _s0=sl0, _s1=sl1):
                                    for dc in range(DC):
                                        i = e.matmul(
                                            _pp[:, :],
                                            lhsT=xhat[_gp][:, dc, _tc * 128:(_tc + 1) * 128],
                                            rhs=wpb[_s0 if dc < 4 else _s1][:, dc % 4,
                                                             _eh * 512:(_eh + 1) * 512],
                                            start=(dc == 0), stop=(dc == DC - 1))
                                    return i
                                tr.op("PE", _vg,
                                      r=[f"xhat{gp}.{dc}" for dc in range(DC)]
                                      + [f"wpb{sl0}", f"wpb{sl1}", pk], w=[pk])
                                tr.op("ACT", lambda e, _s=slot, _eh=eh, _pp=pp: e.activation(
                                    out=vb[:, _s, _eh * 512:(_eh + 1) * 512], in_=_pp[:, :],
                                    func=AF.Copy),
                                    r=[pk], w=[f"vb.{slot}"])
                    else:
                        dst = kb if pj == "k" else qb
                        dkey = pj + "b"
                        for h in range(NH):
                            hp = h % 2
                            pp, pk = (psP0, "psP0") if hp == 0 else (psP1, "psP1")

                            def _qg(e, _h=h, _pp=pp, _gp=gp, _s0=sl0, _s1=sl1):
                                for dc in range(DC):
                                    i = e.matmul(
                                        _pp[:, :],
                                        lhsT=wpb[_s0 if dc < 4 else _s1][:, dc % 4,
                                                                         _h * 128:(_h + 1) * 128],
                                        rhs=xhat[_gp][:, dc, :],
                                        start=(dc == 0), stop=(dc == DC - 1))
                                return i
                            tr.op("PE", _qg,
                                  r=[f"xhat{gp}.{dc}" for dc in range(DC)]
                                  + [f"wpb{sl0}", f"wpb{sl1}", pk], w=[pk])
                            tr.op("ACT", lambda e, _pp=pp, _hp=hp: e.activation(
                                out=ropeA[_hp][:, :], in_=_pp[:, :], func=AF.Copy),
                                r=[pk], w=[f"ropeA{hp}"])
                            tr.op("PE", lambda e, _hp=hp: e.matmul(
                                psStat[:, :], lhsT=rotT_s[:, :], rhs=ropeA[_hp][:, :],
                                start=True, stop=True),
                                r=["c:rot", f"ropeA{hp}"],
                                w=["psStat", "st:mrep", "st:hoist"])
                            tr.op("DVE", lambda e, _hp=hp, _gp=gp: e.tensor_mul(
                                tmp1[_hp][:, :], ropeA[_hp][:, :], ct_s[_gp][:, :]),
                                r=[f"ropeA{hp}", f"ct{gp}"], w=[f"tmp1{hp}"])
                            tr.op("DVE", lambda e, _hp=hp, _gp=gp: e.tensor_mul(
                                tmp2[_hp][:, :], psStat[:, :], st_s[_gp][:, :]),
                                r=["psStat", f"st{gp}"], w=[f"tmp2{hp}"])
                            tr.op("DVE", lambda e, _d=dst, _h=h, _gp=gp, _hp=hp: e.tensor_add(
                                _d[:, _h, _gp * 512:(_gp + 1) * 512],
                                tmp1[_hp][:, :], tmp2[_hp][:, :]),
                                r=[f"tmp1{hp}", f"tmp2{hp}"], w=[f"{dkey}.{h}.g{g}"])

            def scanwout(g):
                gp = g % 2
                for lc in range(4):
                    cg = g * 4 + lc
                    cp = cg % 2
                    slot = cg % 8
                    csl = slice(lc * 128, (lc + 1) * 128)
                    rsl = slice(gp * 512 + lc * 128, gp * 512 + (lc + 1) * 128)
                    tr.op("ACT", lambda e, _gp=gp, _lc=lc: e.activation(
                        out=egml_s[:, :], in_=tmg_s[_gp][:, _lc * 16:_lc * 16 + 8],
                        func=AF.Exp),
                        r=[f"tmg{gp}", "uegate"], w=["egml"])
                    tr.op("DVE", lambda e, _gp=gp, _lc=lc: e.tensor_add(
                        ue8_s[:, :], tmg_s[_gp][:, _lc * 16:_lc * 16 + 8],
                        tmg_s[_gp][:, _lc * 16 + 8:_lc * 16 + 16]),
                        r=[f"tmg{gp}", "uegate"], w=["ue8"])
                    tr.op("ACT", lambda e: e.activation(out=ue8_s[:, :], in_=ue8_s[:, :],
                                                        func=AF.Exp),
                          r=["ue8"], w=["ue8"])
                    tr.op("ACT", lambda e, _gp=gp, _lc=lc: e.activation(
                        out=ac8_s[:, :], in_=tmg_s[_gp][:, _lc * 16 + 8:_lc * 16 + 16],
                        func=AF.Exp),
                        r=[f"tmg{gp}"], w=["ac8"])
                    tr.op("DVE", lambda e, _gp=gp, _csl=csl: e.tensor_mul(
                        em8_s[:, :].rearrange("p (h t) -> p h t", h=NH),
                        ehat_s[_gp][:, _csl].rearrange("p (o t) -> p o t", o=1)
                        .broadcast_to([8, NH, 128]),
                        hmask_s[:, :].rearrange("p (h t) -> p h t", h=NH)),
                        r=[f"ehat{gp}", "c:hmask"], w=["em8"])

                    def _bc(e):
                        for bk in range(2):
                            i = e.matmul(psE[:, bk * 512:(bk + 1) * 512],
                                         lhsT=onesb_s[0:8, :],
                                         rhs=em8_s[:, bk * 512:(bk + 1) * 512],
                                         start=True, stop=True)
                        return i
                    tr.op("PE", _bc, r=["em8", "c:onesb", "psE"], w=["psE"])
                    tr.op("DVE", lambda e, _cp=cp, _rsl=rsl: e.tensor_mul(
                        qtl[_cp][:, :, :],
                        qb[:, :, _rsl],
                        psE[:, :].rearrange("p (h t) -> p h t", h=NH)),
                        r=[f"qb.{h}.g{g}" for h in range(NH)] + ["psE"],
                        w=[f"qtl{cp}"])

                    def _ska(e, _cp=cp, _rsl=rsl):
                        for h in range(NH):
                            e.transpose(psXK[:, h * 128:(h + 1) * 128],
                                        kb[:, h, _rsl], eyeb_s[:, :])
                            i = e.matmul(psA[:, h * 128:(h + 1) * 128],
                                         lhsT=kb[:, h, _rsl], rhs=qtl[_cp][:, h, :],
                                         start=True, stop=True)
                        return i
                    tr.op("PE", _ska,
                          r=[f"kb.{h}.g{g}" for h in range(NH)]
                          + [f"qtl{cp}", "c:eyeb", "psXK", "psA"],
                          w=["psXK", "psA"])
                    tr.op("DVE", lambda e: e.tensor_mul(
                        pbt_s[:, :, :],
                        psA[:, :].rearrange("p (h t) -> p h t", h=NH),
                        egml_s[:, :].rearrange("p (h o) -> p h o", o=1)
                        .broadcast_to([128, NH, 128])),
                        r=["psA", "egml"], w=["pbt"])
                    tr.op("DVE", lambda e: e.tensor_mul(
                        pbt_s[:, :, :].rearrange("p h t -> p (h t)"),
                        pbt_s[:, :, :].rearrange("p h t -> p (h t)"),
                        mask8_s[:, :]),
                        r=["pbt", "c:mask8"], w=["pbt"])
                    tr.op("ACT", lambda e: e.activation(
                        out=ktm_s[:, :, :].rearrange("p h t -> p (h t)"), in_=psXK[:, :],
                        func=AF.Copy),
                        r=["psXK"], w=["ktm"])
                    if cg < NCH - 1:
                        def _vtg(e, _s=slot):
                            for h in range(NH):
                                i = e.tensor_scalar(
                                    out=vtl_s[:, h, :],
                                    in0=vb[:, _s, h * 128:(h + 1) * 128],
                                    scalar1=ue8_s[:, h:h + 1], scalar2=None, op0=ALU.mult)
                            return i
                        tr.op("DVE", _vtg, r=[f"vb.{slot}", "ue8"], w=["vtl", "uegate"])

                        def _kvg(e):
                            for h in range(NH):
                                i = e.matmul(psA[:, h * 128:(h + 1) * 128],
                                             lhsT=ktm_s[:, h, :], rhs=vtl_s[:, h, :],
                                             start=True, stop=True)
                            return i
                        tr.op("PE", _kvg, r=["ktm", "vtl", "pbt", "psA"],
                              w=["psA"])
                    if cg > 0:
                        def _yinter(e, _cp=cp):
                            for h in range(NH):
                                i = e.matmul(psE[:, h * 128:(h + 1) * 128],
                                             lhsT=mbb_s[:, h, :], rhs=qtl[_cp][:, h, :],
                                             start=True, stop=True)
                            return i
                        tr.op("PE", _yinter, r=["mbb", f"qtl{cp}", "psE"], w=["psE"])
                    if cg < NCH - 1:
                        if cg == 0:
                            for bk in range(2):
                                tr.op("DVE", lambda e, _b=bk: e.tensor_copy(
                                    out=msb_s[:, 4 * _b:4 * (_b + 1), :],
                                    in_=psA[:, _b * 512:(_b + 1) * 512]
                                    .rearrange("p (h t) -> p h t", h=4)),
                                    r=["psA"], w=["msb"])
                        else:
                            def _mdec(e):
                                for h in range(NH):
                                    i = e.tensor_scalar(
                                        out=msb_s[:, h, :], in0=msb_s[:, h, :],
                                        scalar1=ac8_s[:, h:h + 1], scalar2=None,
                                        op0=ALU.mult)
                                return i
                            tr.op("DVE", _mdec, r=["msb", "ac8"], w=["msb"])
                            for bk in range(2):
                                tr.op("DVE", lambda e, _b=bk: e.tensor_add(
                                    msb_s[:, 4 * _b:4 * (_b + 1), :],
                                    msb_s[:, 4 * _b:4 * (_b + 1), :],
                                    psA[:, _b * 512:(_b + 1) * 512]
                                    .rearrange("p (h t) -> p h t", h=4)),
                                    r=["msb", "psA"], w=["msb"])
                        tr.op("ACT", lambda e: e.activation(
                            out=mbb_s[:, :, :].rearrange("p h t -> p (h t)"),
                            in_=msb_s[:, :, :].rearrange("p h t -> p (h t)"), func=AF.Copy),
                            r=["msb"], w=["mbb"])
                    def _yintra(e, _s=slot):
                        for h in range(NH):
                            i = e.matmul(psA[:, h * 128:(h + 1) * 128],
                                         lhsT=vb[:, _s, h * 128:(h + 1) * 128],
                                         rhs=pbt_s[:, h, :],
                                         start=True, stop=True)
                        return i
                    tr.op("PE", _yintra, r=[f"vb.{slot}", "pbt", "psA"], w=["psA"])
                    if cg > 0:
                        for bk in range(2):
                            tr.op("DVE", lambda e, _lc=lc, _b=bk: e.tensor_copy(
                                out=yring[:, 4 * _b:4 * (_b + 1), _lc * 128:(_lc + 1) * 128],
                                in_=psE[:, _b * 512:(_b + 1) * 512]
                                .rearrange("p (h t) -> p h t", h=4)),
                                r=["psE"], w=[f"yring.{lc}"])
                        for bk in range(2):
                            tr.op("DVE", lambda e, _lc=lc, _b=bk: e.tensor_add(
                                yring[:, 4 * _b:4 * (_b + 1), _lc * 128:(_lc + 1) * 128],
                                yring[:, 4 * _b:4 * (_b + 1), _lc * 128:(_lc + 1) * 128],
                                psA[:, _b * 512:(_b + 1) * 512]
                                .rearrange("p (h t) -> p h t", h=4)),
                                r=["psA", f"yring.{lc}"], w=[f"yring.{lc}"])
                    else:
                        for bk in range(2):
                            tr.op("DVE", lambda e, _lc=lc, _b=bk: e.tensor_copy(
                                out=yring[:, 4 * _b:4 * (_b + 1), _lc * 128:(_lc + 1) * 128],
                                in_=psA[:, _b * 512:(_b + 1) * 512]
                                .rearrange("p (h t) -> p h t", h=4)),
                                r=["psA"], w=[f"yring.{lc}"])
                for wh in range(2):
                    for dcp in range(DC):
                        pp, pk = (psP0, "psP0") if dcp % 2 == 0 else (psP1, "psP1")

                        def _wog(e, _dcp=dcp, _pp=pp, _wh=wh):
                            for h in range(NH):
                                i = e.matmul(
                                    _pp[:, 0:256],
                                    lhsT=woutb[:, h, _dcp * 128:(_dcp + 1) * 128],
                                    rhs=yring[:, h, _wh * 256:(_wh + 1) * 256],
                                    start=(h == 0), stop=(h == NH - 1))
                            return i
                        tr.op("PE", _wog,
                              r=["woutb"] + [f"yring.{i}" for i in range(4)] + [pk], w=[pk])
                        tr.op("DVE", lambda e, _dcp=dcp, _pp=pp: e.tensor_scalar(
                            out=aosl[:, _dcp, :], in0=_pp[:, 0:256],
                            scalar1=bout_s[:, _dcp:_dcp + 1], scalar2=None, op0=ALU.add),
                            r=[pk, "c:bout"], w=["aosl"])
                    tr.dma("POOLQ", "d_ao",
                           lambda e, _g=g, _wh=wh: e.dma_start(
                               out=x1h.rearrange("(c p) t -> p c t", p=128)
                               [:, :, _g * 512 + _wh * 256:_g * 512 + (_wh + 1) * 256],
                               in_=aosl[:, :, :], accum_op=ALU.add),
                           r=["aosl", f"x1h.{g}"], w=[f"x1h.{g}"])

            frontS(0)
            if upto < 2:
                raise _Done()
            for g in range(NG):
                proj(g)
                if upto < 3:
                    raise _Done()
                frontG(g)
                if g == 0:
                    tr.dma("ACTQ", "d_x1init", lambda e: e.dma_start(out=x1h, in_=xTf),
                           w=[f"x1h.{gg}" for gg in range(NG)])
                if g + 1 < NG:
                    frontS(g + 1)
                if g > 0:
                    scanwout(g - 1)
            scanwout(NG - 1)

        tr.barrier()
        with ExitStack() as ph6:
            _e6 = ph6.enter_context
            wib = _e6(nc.sbuf_tensor("wibuf", [128, DC, FF], BF16))
            wob = _e6(nc.sbuf_tensor("wobuf", [128, FC, D], BF16))
            bcat = _e6(nc.sbuf_tensor("bcat", [32, FF], BF16))
            gab = _e6(nc.sbuf_tensor("gab", [128, DC, 8], BF16))
            aab = _e6(nc.sbuf_tensor("aab", [128, DC, 32], BF16))
            x1sl = _e6(nc.sbuf_tensor("x1sl", [128, DC, 512], BF16))
            fsc = _e6(nc.sbuf_tensor("fsc", [128, 512], BF16))
            xnb = [_e6(nc.sbuf_tensor(f"xnb{i}", [128, DC, 512], BF16)) for i in range(2)]
            inter = _e6(nc.sbuf_tensor("inter", [128, FC, 512], BF16))
            x2sl = [_e6(nc.sbuf_tensor("x2sl0", [128, 512], F32))] * 2
            sqscr = _e6(nc.sbuf_tensor("sqscr", [128, 512], BF16))
            m2bc = _e6(nc.sbuf_tensor("m2bc", [128, 512], F32))
            v2bc = _e6(nc.sbuf_tensor("v2bc", [128, 512], F32))
            rp8 = _e6(nc.sbuf_tensor("rp8", [8, 512], F32))
            pp32 = _e6(nc.sbuf_tensor("pp32", [32, 512], BF16))
            tm8 = _e6(nc.sbuf_tensor("tm8", [128, 32], F32))
            mx8 = _e6(nc.sbuf_tensor("mx8", [128, 8], F32))
            ix8 = _e6(nc.sbuf_tensor("ix8", [128, 8], U32))
            ixf = _e6(nc.sbuf_tensor("ixf", [128, 4], F32))
            m8t = _e6(nc.sbuf_tensor("m8t", [128, 32], F32))
            m8f = _e6(nc.sbuf_tensor("m8f", [8, 512], BF16))
            psS1 = _e6(nc.psum_tensor("psS1", [128, 512], F32))
            psS2 = _e6(nc.psum_tensor("psS2", [128, 512], F32))
            psR8 = _e6(nc.psum_tensor("psR8", [128, 512], F32))
            psP32 = _e6(nc.psum_tensor("psP32", [32, 512], F32))
            psF0 = _e6(nc.psum_tensor("psF0", [128, 512], F32))
            psF1 = _e6(nc.psum_tensor("psF1", [128, 512], F32))
            psO0 = _e6(nc.psum_tensor("psO0", [128, 512], F32))
            psO1 = _e6(nc.psum_tensor("psO1", [128, 512], F32))
            if upto < 5:
                raise _Done()
            tr.dma("POOLQ", "d_ga",
                   lambda e: e.dma_start(out=gab[:, :, :],
                                         in_=gw8T.rearrange("(c p) e -> p c e", p=128)),
                   w=["gab"])
            tr.dma("POOLQ", "d_aa",
                   lambda e: e.dma_start(out=aab[:, :, :],
                                         in_=acatT.rearrange("(c p) e -> p c e", p=128)),
                   w=["aab"])
            tr.dma("POOLQ", "d_bcat", lambda e: e.dma_start(out=bcat[:, :], in_=bcatT),
                   w=["bcat"])

            def load_ffn_weights():
                for i in range(4):
                    tr.dma("POOLQ", f"d_wi{i}",
                           lambda e, _i=i: e.dma_start(
                               out=wib[:, :, _i * 1024:(_i + 1) * 1024],
                               in_=wiT.rearrange("(c p) f -> p c f", p=128)
                               [:, :, _i * 1024:(_i + 1) * 1024]),
                           w=[f"wib.{i}"])
                for i in range(4):
                    tr.dma("POOLQ", f"d_wo{i}",
                           lambda e, _i=i: e.dma_start(
                               out=wob[:, 8 * _i:8 * _i + 8, :],
                               in_=woT.rearrange("(c p) f -> p c f", p=128)[:, 8 * _i:8 * _i + 8, :]),
                           w=[f"wob.{i}"])

            def frontA(g):
                bb = g % 2
                tr.dma("POOLQ", "d_x1sl",
                       lambda e, _g=g: e.dma_start(
                           out=x1sl[:, :, :],
                           in_=x1h.rearrange("(c p) t -> p c t", p=128)
                           [:, :, _g * 512:(_g + 1) * 512]),
                       r=[f"x1h.{g}"], w=["x1sl"])
                tr.dma("SYNCQ", f"d_oinit{g}",
                       lambda e, _g=g: e.dma_start(
                           out=outT[:, _g * 512:(_g + 1) * 512],
                           in_=x1h[:, _g * 512:(_g + 1) * 512]),
                       r=[f"x1h.{g}"], w=[f"outT.{g}"])

                def _s1g(e):
                    for dc in range(DC):
                        i = e.matmul(psS1[:, :], lhsT=onesb_s[:, :], rhs=x1sl[:, dc, :],
                                     start=(dc == 0), stop=(dc == DC - 1))
                    return i
                tr.op("PE", _s1g, r=["c:onesb", "x1sl", "psS1"], w=["psS1"])
                for dc in range(DC):
                    tr.op("ACT", lambda e, _dc=dc: e.activation(out=sqscr[:, :],
                                                                in_=x1sl[:, _dc, :],
                                                                func=AF.Square),
                          r=["x1sl"], w=["sqscr"])
                    tr.op("PE", lambda e, _dc=dc: e.matmul(
                        psS2[:, :], lhsT=onesb_s[:, :], rhs=sqscr[:, :],
                        start=(_dc == 0), stop=(_dc == DC - 1)),
                        r=["c:onesb", "sqscr", "psS2"], w=["psS2"])
                tr.op("ACT", lambda e: e.activation(out=m2bc[:, :], in_=psS1[:, :],
                                                    func=AF.Copy, scale=1.0 / D),
                      r=["psS1"], w=["m2bc"])
                tr.op("ACT", lambda e: e.activation(out=v2bc[:, :], in_=psS2[:, :],
                                                    func=AF.Copy, scale=1.0 / D),
                      r=["psS2"], w=["v2bc"])
                tr.op("DVE", lambda e: e.tensor_mul(fsc[:, :], m2bc[:, :], m2bc[:, :]),
                      r=["m2bc"], w=["fsc"])
                tr.op("DVE", lambda e: e.tensor_sub(v2bc[:, :], v2bc[:, :], fsc[:, :]),
                      r=["v2bc", "fsc"], w=["v2bc"])
                tr.op("ACT", lambda e: e.activation(out=v2bc[:, :], in_=v2bc[:, :],
                                                    func=AF.Ln, bias=cb_s[:, 2:3]),
                      r=["v2bc", "c:cb"], w=["v2bc"])
                tr.op("ACT", lambda e: e.activation(out=v2bc[:, :], in_=v2bc[:, :],
                                                    func=AF.Exp, scale=-0.5),
                      r=["v2bc"], w=["v2bc"])
                for dc in range(DC):
                    tr.op("DVE", lambda e, _dc=dc: e.tensor_sub(fsc[:, :], x1sl[:, _dc, :],
                                                                m2bc[:, :]),
                          r=["x1sl", "m2bc"], w=["fsc"])
                    tr.op("DVE", lambda e, _dc=dc, _bb=bb: e.tensor_mul(
                        xnb[_bb][:, _dc, :], fsc[:, :], v2bc[:, :]),
                        r=["fsc", "v2bc"], w=[f"xnb{bb}"])

            def frontB(g):
                bb = g % 2

                def _rg(e, _bb=bb):
                    for dc in range(DC):
                        e.matmul(psR8[0:8, :], lhsT=gab[:, dc, :], rhs=xnb[_bb][:, dc, :],
                                 start=(dc == 0), stop=(dc == DC - 1))
                        i = e.matmul(psP32[:, :], lhsT=aab[:, dc, :], rhs=xnb[_bb][:, dc, :],
                                     start=(dc == 0), stop=(dc == DC - 1))
                    return i
                tr.op("PE", _rg, r=["gab", "aab", f"xnb{bb}", "psR8", "psP32"],
                      w=["psR8", "psP32"])
                tr.op("DVE", lambda e: e.tensor_scalar(out=rp8[:, :], in0=psR8[0:8, :],
                                                       scalar1=g8_s[:, :], scalar2=None,
                                                       op0=ALU.add),
                      r=["psR8", "c:g8"], w=["rp8"])
                tr.op("ACT", lambda e: e.activation(out=pp32[:, :], in_=psP32[:, :],
                                                    func=AF.Copy),
                      r=["psP32"], w=["pp32"])
                for tc4 in range(4):
                    tr.op("PE", lambda e, _tc=tc4: e.transpose(
                        psS1[:, _tc * 8:(_tc + 1) * 8], rp8[:, _tc * 128:(_tc + 1) * 128],
                        eyef_s[0:8, 0:8]),
                        r=["rp8", "c:eyef", "psS1"], w=["psS1"])
                pe_settle(psS1[32:34, 40:42], ["psS1", "rp8"])
                tr.op("ACT", lambda e: e.activation(
                    out=tm8[:, :], in_=psS1[:, 0:32], func=AF.Copy),
                    r=["psS1"], w=["tm8"])
                for tc4 in range(4):
                    tr.op("DVE", lambda e, _tc=tc4: e.max(
                        out=mx8[:, :], in_=tm8[:, _tc * 8:(_tc + 1) * 8]),
                        r=["tm8"], w=["mx8"])
                    tr.op("DVE", lambda e, _tc=tc4: e.max_index(
                        out=ix8[:, :], in_max=mx8[:, :],
                        in_values=tm8[:, _tc * 8:(_tc + 1) * 8]),
                        r=["mx8", "tm8"], w=["ix8"])
                    tr.op("DVE", lambda e, _tc=tc4: e.tensor_copy(
                        out=ixf[:, _tc:_tc + 1], in_=ix8[:, 0:1]),
                        r=["ix8"], w=["ixf"])
                    tr.op("DVE", lambda e, _tc=tc4: e.tensor_scalar(
                        out=m8t[:, _tc * 8:(_tc + 1) * 8], in0=iota8_s[:, :],
                        scalar1=ixf[:, _tc:_tc + 1], scalar2=None, op0=ALU.is_equal),
                        r=["c:iota", "ixf"], w=["m8t"])
                for tc4 in range(4):
                    tr.op("PE", lambda e, _tc=tc4: e.transpose(
                        psR8[0:8, _tc * 128:(_tc + 1) * 128], m8t[:, _tc * 8:(_tc + 1) * 8],
                        eyef_s[:, :]),
                        r=["m8t", "c:eyef", "psR8"], w=["psR8"])
                pe_settle(psR8[32:34, 40:42], ["psR8", "m8t"])
                tr.op("ACT", lambda e: e.activation(
                    out=m8f[:, :], in_=psR8[0:8, :], func=AF.Copy),
                    r=["psR8"], w=["m8f"])
                tr.op("PE", lambda e: e.matmul(psP32[:, :], lhsT=expand_s[:, :], rhs=m8f[:, :],
                                               start=True, stop=True),
                      r=["c:exp", "m8f", "psP32"], w=["psP32"])
                tr.op("DVE", lambda e: e.tensor_mul(pp32[:, :], pp32[:, :], psP32[:, :]),
                      r=["pp32", "psP32"], w=["pp32"])

            def body_wi(g):
                bb = g % 2
                for fc in range(FC):
                    pp, pk = (psF0, "psF0") if fc % 2 == 0 else (psF1, "psF1")

                    def _wig(e, _fc=fc, _pp=pp, _bb=bb):
                        for dc in range(DC):
                            e.matmul(_pp[:, :],
                                     lhsT=wib[:, dc, _fc * 128:(_fc + 1) * 128],
                                     rhs=xnb[_bb][:, dc, :], start=(dc == 0), stop=False)
                        return e.matmul(_pp[:, :],
                                        lhsT=bcat[:, _fc * 128:(_fc + 1) * 128],
                                        rhs=pp32[:, :], start=False, stop=True)
                    tr.op("PE", _wig,
                          r=[f"wib.{fc // 8}", f"xnb{bb}", "bcat", "pp32", pk],
                          w=[pk])
                    tr.op("ACT", lambda e, _fc=fc, _pp=pp: e.activation(
                        out=inter[:, _fc, :], in_=_pp[:, :], func=AF.Relu,
                        bias=wib_s[:, _fc:_fc + 1]),
                        r=[pk, "c:wib"], w=[f"inter.{fc}"])

            def body_wo(g, d0, d1):
                for dcp in range(d0, d1):
                    pp, pk = (psO0, "psO0") if dcp % 2 == 0 else (psO1, "psO1")

                    def _wog2(e, _dcp=dcp, _pp=pp):
                        for fc in range(FC):
                            i = e.matmul(_pp[:, :],
                                         lhsT=wob[:, fc, _dcp * 128:(_dcp + 1) * 128],
                                         rhs=inter[:, fc, :], start=(fc == 0),
                                         stop=(fc == FC - 1))
                        return i
                    tr.op("PE", _wog2,
                          r=[f"wob.{i}" for i in range(4)]
                          + [f"inter.{fc}" for fc in range(FC)] + [pk],
                          w=[pk])
                    tr.op("DVE", lambda e, _dcp=dcp, _pp=pp: e.tensor_scalar(
                        out=x2sl[0][:, :], in0=_pp[:, :], scalar1=wob_s[:, _dcp:_dcp + 1],
                        scalar2=None, op0=ALU.add),
                        r=[pk, "c:wob", "x2sl0"], w=["x2sl0"])
                    tr.dma("POOLQ", "d_out0",
                           lambda e, _g=g, _dcp=dcp: e.dma_start(
                               out=outT[_dcp * 128:(_dcp + 1) * 128,
                                        _g * 512:(_g + 1) * 512],
                               in_=x2sl[0][:, :], accum_op=ALU.add),
                           r=["x2sl0", f"outT.{g}"], w=[f"outT.{g}"])

            frontA(0)
            frontB(0)
            load_ffn_weights()
            for g in range(NG):
                body_wi(g)
                if g + 1 < NG:
                    frontA(g + 1)
                body_wo(g, 0, 4)
                if g + 1 < NG:
                    frontB(g + 1)
                body_wo(g, 4, 8)
        raise _Done()
    except _Done:
        with ExitStack() as c2:
            tr.final_wait("POOLQ", "d_out0")
            tr.build_blocks(nc, c2)
    return nc


_NC_CACHE = {}


def _get_nc():
    if "nc" not in _NC_CACHE:
        _NC_CACHE["nc"] = build()
    return _NC_CACHE["nc"]


def kernel(**inputs):
    x = np.asarray(inputs["L0_attn_x"])
    nc = _get_nc()
    shared = _shared_arrays(inputs)
    in_maps = []
    for c in range(8):
        m = dict(shared)
        xt = np.ascontiguousarray(x[c].T)
        m["xT"] = xt
        m["xTf"] = xt
        in_maps.append(m)
    res = run_bass_kernel_spmd(nc, in_maps, core_ids=list(range(8)))
    return np.stack([np.asarray(r["outT"]).T for r in res.results], axis=0)


def _shared_arrays(inputs):
    ii = {k: np.asarray(v) for k, v in inputs.items()}
    wgP = np.zeros((D, 128), np.float32)
    wgP[:, 0:8] = ii["L0_attn_Wbeta"].T
    wgP[:, 32:40] = ii["L0_attn_Wig"].T
    wgP[:, 64:72] = ii["L0_attn_Wog"].T
    b128 = np.zeros((128, 1), np.float32)
    b128[0:8, 0] = ii["L0_attn_bbeta"]
    b128[32:40, 0] = ii["L0_attn_big"]
    b128[64:72, 0] = ii["L0_attn_bog"]
    return {
        "wqT": np.ascontiguousarray(ii["L0_attn_Wq"].T),
        "wkT": np.ascontiguousarray(ii["L0_attn_Wk"].T),
        "wvT": np.ascontiguousarray(ii["L0_attn_Wv"].T),
        "woutT": np.ascontiguousarray(ii["L0_attn_Wout"].T),
        "wgP": wgP, "b128": b128,
        "nw8": np.ascontiguousarray(ii["L0_attn_norm_w"].reshape(DC, 128).T),
        "gw8T": np.ascontiguousarray(ii["L0_ffn_gate_W"].T),
        "acatT": np.ascontiguousarray(
            np.transpose(ii["L0_ffn_lora_A"], (2, 0, 1)).reshape(D, 32)),
        "g8": np.ascontiguousarray(ii["L0_ffn_gate_b"].reshape(8, 1)),
        "bcatT": np.ascontiguousarray(
            np.transpose(ii["L0_ffn_lora_B"], (0, 2, 1)).reshape(32, FF)),
        "wiT": np.ascontiguousarray(ii["L0_ffn_wi_W"].T),
        "woT": np.ascontiguousarray(ii["L0_ffn_wo_W"].T),
        "wib32": np.ascontiguousarray(ii["L0_ffn_wi_b"].reshape(FC, 128).T),
        "wob8": np.ascontiguousarray(ii["L0_ffn_wo_b"].reshape(DC, 128).T),
        "bout8": np.ascontiguousarray(ii["L0_attn_bout"].reshape(DC, 128).T),
    }


# version 21
# speedup vs baseline: 1.0403x; 1.0403x over previous
import numpy as np
from contextlib import ExitStack

import concourse.bass as bass
import concourse.mybir as mybir
from concourse.bass_utils import run_bass_kernel_spmd

F32 = mybir.dt.float32
F32R = mybir.dt.float32r
BF16 = mybir.dt.bfloat16
U32 = mybir.dt.uint32
AF = mybir.ActivationFunctionType
ALU = mybir.AluOpType
AX = mybir.AxisListType

T, D, NH, DH, FF = 2048, 1024, 8, 128, 4096
NCH, NG, DC, FC = 16, 4, 8, 32
SCALE = 1.0 / float(np.sqrt(DH))

EMAP = {"PE": "tensor", "DVE": "vector", "ACT": "scalar", "POOL": "gpsimd",
        "SYNCQ": "sync", "ACTQ": "scalar", "POOLQ": "gpsimd"}
ENGS = ("PE", "DVE", "ACT", "POOL")


class TR:

    def __init__(self):
        self.cnt = {}
        self.lastw = {}
        self.readers = {}
        self.waited = {}
        self.floor = {}
        self.ops = {k: [] for k in ("tensor", "vector", "scalar", "gpsimd", "sync")}
        self.sem_names = set(ENGS)

    def barrier(self):
        self.floor = dict(self.cnt)

    def _wl(self, stream, r, w):
        need = dict(self.floor)
        for k in r:
            lw = self.lastw.get(k)
            if lw:
                need[lw[0]] = max(need.get(lw[0], 0), lw[1])
        for k in w:
            lw = self.lastw.get(k)
            if lw:
                need[lw[0]] = max(need.get(lw[0], 0), lw[1])
            for s2, v2 in self.readers.get(k, {}).items():
                need[s2] = max(need.get(s2, 0), v2)
        out = []
        wt = self.waited.setdefault(stream, {})
        for s2 in sorted(need):
            if wt.get(s2, 0) < need[s2]:
                wt[s2] = need[s2]
                out.append((s2, need[s2]))
        return out

    def _commit(self, sem, val, r, w):
        for k in w:
            self.lastw[k] = (sem, val)
            self.readers[k] = {}
        for k in r:
            self.readers.setdefault(k, {})[sem] = val

    def op(self, eng, fn, r=(), w=()):
        wl = self._wl(eng, r, w)
        self.cnt[eng] = self.cnt.get(eng, 0) + 1
        val = self.cnt[eng]

        def emit(engobj, sems, _wl=wl, _fn=fn, _eng=eng):
            for s2, v2 in _wl:
                engobj.wait_ge(sems[s2], v2)
            _fn(engobj).then_inc(sems[_eng], 1)
        self.ops[EMAP[eng]].append(emit)
        self._commit(eng, val, r, w)

    def dma(self, q, dsem, fn, r=(), w=()):
        self.sem_names.add(dsem)
        wl = self._wl(q, r, w)
        self.cnt[dsem] = self.cnt.get(dsem, 0) + 16
        val = self.cnt[dsem]

        def emit(engobj, sems, _wl=wl, _fn=fn, _ds=dsem):
            for s2, v2 in _wl:
                engobj.wait_ge(sems[s2], v2)
            _fn(engobj).then_inc(sems[_ds], 16)
        self.ops[EMAP[q]].append(emit)
        self._commit(dsem, val, r, w)

    def final_wait(self, stream, sem):
        v = self.cnt.get(sem, 0)
        self.ops[EMAP[stream]].append(
            lambda e, s, _sm=sem, _v=v: e.wait_ge(s[_sm], _v) if _v else None)

    def build_blocks(self, nc, ctx):
        sems = {n: ctx.enter_context(nc.semaphore(f"s_{n}"))
                for n in sorted(self.sem_names)}
        with nc.Block() as block:
            @block.sync
            def _(eng):
                for emit in self.ops["sync"]:
                    emit(eng, sems)

            @block.tensor
            def _(eng):
                for emit in self.ops["tensor"]:
                    emit(eng, sems)

            @block.vector
            def _(eng):
                for emit in self.ops["vector"]:
                    emit(eng, sems)

            @block.scalar
            def _(eng):
                for emit in self.ops["scalar"]:
                    emit(eng, sems)

            @block.gpsimd
            def _(eng):
                for emit in self.ops["gpsimd"]:
                    emit(eng, sems)


def build(upto=9):
    import ml_dtypes
    nc = bass.Bass("TRN2", target_bir_lowering=False, debug=False)

    def din(name, shape, dt=F32):
        return nc.dram_tensor(name, shape, dt, kind="ExternalInput").ap()
    xT = din("xT", [D, T], F32R)
    xTf = din("xTf", [D, T])
    wqT = din("wqT", [D, D]); wkT = din("wkT", [D, D]); wvT = din("wvT", [D, D])
    woutT = din("woutT", [D, D])
    wgP = din("wgP", [D, 128])
    b128 = din("b128", [128, 1])
    nw8 = din("nw8", [128, DC])
    gw8T = din("gw8T", [D, 8])
    acatT = din("acatT", [D, 32])
    g8 = din("g8", [8, 1])
    bcatT = din("bcatT", [32, FF])
    wiT = din("wiT", [D, FF])
    woT = din("woT", [FF, D])
    wib32 = din("wib32", [128, FC])
    wob8 = din("wob8", [128, DC])
    bout8 = din("bout8", [128, DC])
    outT = nc.dram_tensor("outT", [D, T], F32, kind="ExternalOutput").ap()
    x1h = nc.dram_tensor("x1h", [D, T], F32, kind="Internal").ap()
    wv16 = nc.dram_tensor("wv16", [D, D], BF16, kind="Internal").ap()
    wk16 = nc.dram_tensor("wk16", [D, D], BF16, kind="Internal").ap()
    wq16 = nc.dram_tensor("wq16", [D, D], BF16, kind="Internal").ap()

    eye_f = nc.inline_tensor(np.eye(128, dtype=np.float32), "eye_f").ap()
    eye_b = nc.inline_tensor(np.eye(128, dtype=ml_dtypes.bfloat16), "eye_b").ap()
    ones_f = nc.inline_tensor(np.ones((128, 128), np.float32), "ones_f").ap()
    m01 = (np.arange(128)[:, None] <= np.arange(128)[None, :]).astype(np.float64)
    mask8_np = np.tile(m01, (1, 8)).astype(ml_dtypes.bfloat16)
    mask8_c = nc.inline_tensor(mask8_np, "mask8").ap()
    hm_np = np.zeros((8, 1024), np.float64)
    for h in range(8):
        hm_np[h, h * 128:(h + 1) * 128] = 1.0
    hmask_c = nc.inline_tensor(hm_np.astype(ml_dtypes.bfloat16), "hmask8").ap()
    iota8_c = nc.inline_tensor(
        np.broadcast_to(np.arange(8, dtype=np.float32), (128, 8)).copy(), "iota8").ap()
    exp_np = np.zeros((8, 32), ml_dtypes.bfloat16)
    for e in range(8):
        exp_np[e, e * 4:(e + 1) * 4] = 1.0
    expand_c = nc.inline_tensor(exp_np, "expand832").ap()
    rot = np.zeros((128, 128), np.float64)
    rot[np.arange(64), 64 + np.arange(64)] = -1.0
    rot[64 + np.arange(64), np.arange(64)] = 1.0
    rotT_c = nc.inline_tensor(rot.T.astype(ml_dtypes.bfloat16), "rotT").ap()
    sel = np.zeros((128, 16), np.float32)
    sel[64 + np.arange(8), np.arange(8)] = 1.0
    sel[32 + np.arange(8), 8 + np.arange(8)] = 1.0
    sel_c = nc.inline_tensor(sel, "sel16").ap()
    half = DH // 2
    inv_freq = 1.0 / (10000.0 ** (np.arange(half, dtype=np.float64) / half))
    frq = np.arange(T, dtype=np.float64)[None, :] * inv_freq[:, None]
    cs2 = np.concatenate([np.cos(frq), np.cos(frq)], axis=0).astype(ml_dtypes.bfloat16)
    sn2 = np.concatenate([np.sin(frq), np.sin(frq)], axis=0).astype(ml_dtypes.bfloat16)
    ctab_c = nc.inline_tensor(cs2, "ctab").ap()
    stab_c = nc.inline_tensor(sn2, "stab").ap()
    z8_c = nc.inline_tensor(np.zeros((8, 128), np.float32), "z8").ap()

    tr = TR()

    class _Done(Exception):
        pass

    def ldc(dst, src, key):
        tr.dma("SYNCQ", "dc_" + key.replace(":", ""),
               lambda e, _d=dst, _s=src: e.dma_start(out=_d, in_=_s), w=[key])

    def pe_settle(ps_ap, keys):
        tr.op("PE", lambda e, _p=ps_ap: e.matmul(
            _p, lhsT=onesf_s[0:1, 0:2], rhs=onesf_s[0:1, 0:2],
            start=True, stop=True), r=["c:onesf"], w=list(keys))

    ctx = ExitStack()
    try:
      with ctx:
        SB = lambda n, s, d=F32: ctx.enter_context(nc.sbuf_tensor(n, s, d))
        eyef_s = SB("eyef", [128, 128])
        eyeb_s = SB("eyeb", [128, 128], BF16)
        onesf_s = SB("onesf", [128, 128]); onesr_s = SB("onesr", [128, 128], F32R)
        onesb_s = SB("onesb", [128, 128], BF16)
        iota8_s = SB("iota8s", [128, 8])
        expand_s = SB("expand", [8, 32], BF16)
        nw_s = SB("nw", [128, DC]); b128_s = SB("b128s", [128, 1])
        g8_s = SB("g8s", [8, 1])
        wib_s = SB("wibc", [128, FC]); wob_s = SB("wobc", [128, DC]); bout_s = SB("boutc", [128, DC])
        cb_s = SB("cbias", [128, 4])
        ldc(eyef_s[:, :], eye_f, "c:eyef")
        ldc(eyeb_s[:, :], eye_b, "c:eyeb")
        ldc(onesf_s[:, :], ones_f, "c:onesf")
        ldc(iota8_s[:, :], iota8_c, "c:iota"); ldc(expand_s[:, :], expand_c, "c:exp")
        ldc(nw_s[:, :], nw8, "c:nw"); ldc(b128_s[:, :], b128, "c:b128")
        ldc(g8_s[:, :], g8, "c:g8"); ldc(wib_s[:, :], wib32, "c:wib")
        ldc(wob_s[:, :], wob8, "c:wob"); ldc(bout_s[:, :], bout8, "c:bout")
        for j, v in ((0, 1e-11), (1, 1.0), (2, 1e-5),
                     (3, float(np.log(SCALE)))):
            tr.op("DVE", lambda e, _j=j, _v=v: e.memset(cb_s[:, _j:_j + 1], _v), w=["c:cb"])
        tr.op("DVE", lambda e: e.tensor_copy(out=onesr_s[:, :], in_=onesf_s[:, :]),
              r=["c:onesf"], w=["c:onesr"])
        tr.op("DVE", lambda e: e.tensor_copy(out=onesb_s[:, :], in_=onesf_s[:, :]),
              r=["c:onesf"], w=["c:onesb"])

        with ExitStack() as attn:
            _a = attn.enter_context
            ct_s = [_a(nc.sbuf_tensor(f"cts{i}", [128, 512], BF16)) for i in range(2)]
            st_s = [_a(nc.sbuf_tensor(f"sts{i}", [128, 512], BF16)) for i in range(2)]
            rotT_s = _a(nc.sbuf_tensor("rotTs", [128, 128], BF16))
            mask8_s = _a(nc.sbuf_tensor("mask8s", [128, 1024], BF16))
            hmask_s = _a(nc.sbuf_tensor("hmasks", [8, 1024], BF16))
            sel_s = _a(nc.sbuf_tensor("sel16s", [128, 16], F32))
            dg8_s = _a(nc.sbuf_tensor("dg8s", [8, 32], F32))
            z8_s = _a(nc.sbuf_tensor("z8s", [8, 128], F32))
            xsl3 = [_a(nc.sbuf_tensor(f"xsl3{i}", [128, DC, 256], F32R)) for i in range(2)]
            xsq = _a(nc.sbuf_tensor("xsq", [128, DC, 256], BF16))
            xhat = [_a(nc.sbuf_tensor(f"xhat{i}", [128, DC, 512], BF16)) for i in range(2)]
            rrep_s = _a(nc.sbuf_tensor("rrep", [128, 512], F32))
            t1_s = _a(nc.sbuf_tensor("t1s", [128, 256], F32))
            mrow_s = _a(nc.sbuf_tensor("mrow", [1, 1536], F32))
            mrr_s = _a(nc.sbuf_tensor("mrr", [1, 1024], F32R))
            wg_s = _a(nc.sbuf_tensor("wg", [128, DC, 128], F32R))
            wsg_s = _a(nc.sbuf_tensor("wsg", [1, 128], F32R))
            colt_s = _a(nc.sbuf_tensor("colt", [128, 1], F32))
            zt_s = _a(nc.sbuf_tensor("zt", [128, 256], F32))
            sg1_s = _a(nc.sbuf_tensor("sg1", [128, 256], F32))
            ls_s = _a(nc.sbuf_tensor("ls", [128, 512], F32))
            lhat_s = _a(nc.sbuf_tensor("lhat", [8, 512], F32))
            lgml_s = _a(nc.sbuf_tensor("lgml", [8, 512], F32))
            lbuf_s = _a(nc.sbuf_tensor("lbuf", [8, 512], F32))
            ehat_s = [_a(nc.sbuf_tensor(f"ehat{i}", [8, 512], F32)) for i in range(2)]
            em8_s = _a(nc.sbuf_tensor("em8", [8, 1024], BF16))
            tmg_s = [_a(nc.sbuf_tensor(f"tmg{i}", [128, 64], F32)) for i in range(2)]
            egml_s = _a(nc.sbuf_tensor("egml", [128, 8], F32))
            ue8_s = _a(nc.sbuf_tensor("ue8", [128, 8], F32))
            ac8_s = _a(nc.sbuf_tensor("ac8", [128, 8], F32))
            wpb = [_a(nc.sbuf_tensor(f"wpb{i}", [128, 4, 1024], BF16)) for i in range(3)]
            qb = _a(nc.sbuf_tensor("qb", [128, NH, 1024], BF16))
            kb = _a(nc.sbuf_tensor("kb", [128, NH, 1024], BF16))
            vb = _a(nc.sbuf_tensor("vb", [128, 8, D], BF16))
            ropeA = [_a(nc.sbuf_tensor(f"ropeA{i}", [128, 512], BF16)) for i in range(2)]
            tmp1 = [_a(nc.sbuf_tensor(f"tmp1{i}", [128, 512], BF16)) for i in range(2)]
            tmp2 = [_a(nc.sbuf_tensor(f"tmp2{i}", [128, 512], BF16)) for i in range(2)]
            qtl = [_a(nc.sbuf_tensor(f"qtl{i}", [128, NH, 128], BF16)) for i in range(2)]
            pbt_s = _a(nc.sbuf_tensor("pbt", [128, NH, 128], BF16))
            vtl_s = _a(nc.sbuf_tensor("vtl", [128, NH, 128], BF16))
            ktm_s = _a(nc.sbuf_tensor("ktm", [128, NH, 128], BF16))
            msb_s = _a(nc.sbuf_tensor("msb", [128, NH, DH], F32))
            mbb_s = _a(nc.sbuf_tensor("mbb", [128, NH, DH], BF16))
            yring = _a(nc.sbuf_tensor("yring", [128, NH, 512], BF16))
            aosl = _a(nc.sbuf_tensor("aosl", [128, DC, 256], F32))
            woutb = _a(nc.sbuf_tensor("woutb", [128, NH, D], BF16))
            psP0 = _a(nc.psum_tensor("psP0", [128, 512], F32))
            psP1 = _a(nc.psum_tensor("psP1", [128, 512], F32))
            psStat = _a(nc.psum_tensor("psStat", [128, 512], F32))
            psE = _a(nc.psum_tensor("psE", [128, NH * 128], F32))
            psA = _a(nc.psum_tensor("psA", [128, NH * 128], F32))
            psXK = _a(nc.psum_tensor("psXK", [128, NH * 128], BF16))
            if upto < 1:
                raise _Done()

            ldc(rotT_s[:, :], rotT_c, "c:rot"); ldc(mask8_s[:, :], mask8_c, "c:mask8")
            ldc(hmask_s[:, :], hmask_c, "c:hmask"); ldc(sel_s[:, :], sel_c, "c:sel")
            ldc(z8_s[:, :], z8_c, "c:z8")

            tr.dma("POOLQ", "d_wg",
                   lambda e: e.dma_start(out=wg_s[:, :, :],
                                         in_=wgP.rearrange("(c p) e -> p c e", p=128)),
                   w=["wg"])
            tr.dma("POOLQ", "d_wv16", lambda e: e.dma_start(out=wv16, in_=wvT), w=["wv16"])
            tr.dma("POOLQ", "d_wk16", lambda e: e.dma_start(out=wk16, in_=wkT), w=["wk16"])

            def _fold(e):
                for dc in range(DC):
                    i = e.mul(out=wg_s[:, dc, :], in_=wg_s[:, dc, :],
                              mul=nw_s[:, dc:dc + 1])
                return i
            tr.op("ACT", _fold, r=["wg", "c:nw"], w=["wg"])

            def _wgcs(e):
                for dc in range(DC):
                    i = e.matmul(psE[:, 0:2], lhsT=wg_s[:, dc, :], rhs=onesr_s[:, 0:2],
                                 start=(dc == 0), stop=(dc == DC - 1))
                return i
            tr.op("PE", _wgcs, r=["wg", "c:onesr", "psE"], w=["psE"])
            tr.op("ACT", lambda e: e.activation(out=colt_s[:, :], in_=psE[:, 0:1],
                                                func=AF.Copy),
                  r=["psE"], w=["colt"])
            tr.op("PE", lambda e: e.transpose(psE[0:1, 0:128], colt_s[:, :], eyef_s[:, :]),
                  r=["colt", "c:eyef", "psE"], w=["psE"])
            pe_settle(psE[32:34, 200:202], ["psE"])
            tr.op("ACT", lambda e: e.activation(out=wsg_s[0:1, :], in_=psE[0:1, 0:128],
                                                func=AF.Copy, scale=-1.0),
                  r=["psE"], w=["wsg"])
            tr.dma("POOLQ", "d_wout",
                   lambda e: e.dma_start(out=woutb[:, :, :],
                                         in_=woutT.rearrange("(c p) e -> p c e", p=128)),
                   w=["woutb"])

            MR = {"mrow": 0, "msq": 512, "vrow": 1024}
            RR = {"mrowR": 0, "rrow": 512}

            def mrs(k):
                return mrow_s[0:1, MR[k]:MR[k] + 512]

            def rrs(k):
                return mrr_s[0:1, RR[k]:RR[k] + 512]

            def frontS(g):
                gp = g % 2
                gsl = slice(g * 512, (g + 1) * 512)
                tr.dma("SYNCQ", f"d_ct{gp}",
                       lambda e, _gp=gp, _gsl=gsl: e.dma_start(
                           out=ct_s[_gp][:, :], in_=ctab_c[:, _gsl]),
                       w=[f"ct{gp}"])
                tr.dma("SYNCQ", f"d_st{gp}",
                       lambda e, _gp=gp, _gsl=gsl: e.dma_start(
                           out=st_s[_gp][:, :], in_=stab_c[:, _gsl]),
                       w=[f"st{gp}"])
                for hb in range(2):
                    xg256 = slice(g * 512 + hb * 256, g * 512 + (hb + 1) * 256)
                    tr.dma("SYNCQ", f"d_xsl3{hb}",
                           lambda e, _hb=hb, _s=xg256: e.dma_start(
                               out=xsl3[_hb][:, :, :],
                               in_=xT.rearrange("(c p) t -> p c t", p=128)[:, :, _s]),
                           w=[f"xsl3{hb}"])
                    tr.op("ACT", lambda e, _hb=hb: e.activation(
                        out=xsq[:, :, :].rearrange("p c t -> p (c t)"),
                        in_=xsl3[_hb][:, :, :].rearrange("p c t -> p (c t)"),
                        func=AF.Square),
                        r=[f"xsl3{hb}"], w=["xsq"])

                    def _sums(e, _hb=hb):
                        for dc in range(DC):
                            e.matmul(psStat[0:1, _hb * 256:(_hb + 1) * 256],
                                     lhsT=onesr_s[:, 0:1], rhs=xsl3[_hb][:, dc, :],
                                     start=(dc == 0), stop=(dc == DC - 1))
                        for dc in range(DC):
                            i = e.matmul(psP1[0:1, _hb * 256:(_hb + 1) * 256],
                                         lhsT=onesb_s[:, 0:1], rhs=xsq[:, dc, :],
                                         start=(dc == 0), stop=(dc == DC - 1))
                        return i
                    tr.op("PE", _sums,
                          r=[f"xsl3{hb}", "xsq", "c:onesr", "c:onesb", "psStat", "psP1"],
                          w=["psStat", "st:rows", "psP1"])
                tr.op("ACT", lambda e: e.activation(out=rrs("mrowR"), in_=psStat[0:1, :],
                                                    func=AF.Copy, scale=1.0 / D),
                      r=["st:rows"], w=["r:mrowR"])
                tr.op("ACT", lambda e: e.activation(out=mrs("msq"), in_=rrs("mrowR"),
                                                    func=AF.Square),
                      r=["r:mrowR"], w=["r:msq"])
                tr.op("DVE", lambda e: e.scalar_tensor_tensor(
                    out=mrs("vrow"), in0=psP1[0:1, :], scalar=1.0 / D,
                    in1=mrs("msq"), op0=ALU.mult, op1=ALU.subtract),
                      r=["st:rows", "psP1", "r:msq"], w=["r:vrow"])
                tr.op("ACT", lambda e: e.activation(out=mrs("vrow"), in_=mrs("vrow"),
                                                    func=AF.Ln, scale=1.000001,
                                                    bias=cb_s[0:1, 0:1]),
                      r=["r:vrow", "c:cb"], w=["r:vrow"])
                tr.op("ACT", lambda e: e.activation(out=rrs("rrow"), in_=mrs("vrow"),
                                                    func=AF.Exp, scale=-0.5),
                      r=["r:vrow"], w=["r:rrow"])
                tr.op("PE", lambda e: e.matmul(psP0[:, :], lhsT=onesr_s[0:1, :],
                                               rhs=rrs("rrow"), start=True, stop=True),
                      r=["r:rrow", "c:onesr", "psP0"], w=["psP0"])
                tr.op("ACT", lambda e: e.activation(out=rrep_s[:, :], in_=psP0[:, :],
                                                    func=AF.Copy),
                      r=["psP0"], w=["rrep"])
                tr.op("PE", lambda e: e.matmul(psStat[:, :], lhsT=onesr_s[0:1, :],
                                               rhs=rrs("mrowR"), start=True, stop=True),
                      r=["r:mrowR", "c:onesr", "psStat", "st:rows"], w=["psStat", "st:mrep"])
                for hb in range(2):
                    hsl = slice(hb * 256, (hb + 1) * 256)
                    for dc in range(DC):
                        tr.op("DVE", lambda e, _hb=hb, _dc=dc, _hsl=hsl: e.tensor_sub(
                            t1_s[:, :], xsl3[_hb][:, _dc, :], psStat[:, _hsl]),
                            r=[f"xsl3{hb}", "st:mrep"], w=["t1"])
                        tr.op("DVE", lambda e, _hb=hb, _dc=dc, _hsl=hsl, _gp=gp:
                              e.scalar_tensor_tensor(
                                  out=xhat[_gp][:, _dc, _hsl], in0=t1_s[:, :],
                                  scalar=nw_s[:, _dc:_dc + 1], in1=rrep_s[:, _hsl],
                                  op0=ALU.mult, op1=ALU.mult),
                              r=["t1", "c:nw", "rrep"], w=[f"xhat{gp}.{dc}"])
            def frontG(g):
                gp = g % 2
                gsl = slice(g * 512, (g + 1) * 512)
                for hb in range(2):
                    hsl = slice(hb * 256, (hb + 1) * 256)

                    def _zg(e, _hb=hb):
                        for dc in range(DC):
                            e.matmul(psP0[:, 0:256], lhsT=wg_s[:, dc, :],
                                     rhs=xsl3[_hb][:, dc, :], start=(dc == 0), stop=False)
                        return e.matmul(psP0[:, 0:256], lhsT=wsg_s[0:1, :],
                                        rhs=mrr_s[0:1, _hb * 256:(_hb + 1) * 256],
                                        start=False, stop=True)
                    tr.op("PE", _zg, r=["wg", f"xsl3{hb}", "wsg", "r:mrowR", "psP0"],
                          w=["psP0"])
                    tr.op("DVE", lambda e, _hsl=hsl: e.tensor_mul(zt_s[:, :], psP0[:, 0:256],
                                                                  rrep_s[:, _hsl]),
                          r=["psP0", "rrep"], w=["zt"])
                    tr.op("DVE", lambda e: e.tensor_scalar(out=zt_s[:, :], in0=zt_s[:, :],
                                                           scalar1=b128_s[:, :], scalar2=None,
                                                           op0=ALU.add),
                          r=["zt", "c:b128"], w=["zt"])
                    tr.op("ACT", lambda e: e.activation(out=sg1_s[:, :], in_=zt_s[:, :],
                                                        func=AF.Abs),
                          r=["zt"], w=["sg1"])
                    tr.op("ACT", lambda e: e.activation(out=sg1_s[:, :], in_=sg1_s[:, :],
                                                        func=AF.Exp, scale=-1.0),
                          r=["sg1"], w=["sg1"])
                    tr.op("ACT", lambda e: e.activation(out=sg1_s[:, :], in_=sg1_s[:, :],
                                                        func=AF.Ln, bias=cb_s[:, 1:2]),
                          r=["sg1", "c:cb"], w=["sg1"])
                    tr.op("DVE", lambda e: e.tensor_scalar_min(zt_s[:, :], zt_s[:, :], 0.0),
                          r=["zt"], w=["zt"])
                    tr.op("DVE", lambda e, _hsl=hsl: e.tensor_sub(ls_s[:, _hsl], zt_s[:, :],
                                                                  sg1_s[:, :]),
                          r=["zt", "sg1"], w=[f"ls.{hb}"])
                    tr.op("PE", lambda e, _hsl=hsl: e.matmul(
                        psP1[0:8, 0:256], lhsT=sel_s[:, 0:8], rhs=ls_s[:, _hsl],
                        start=True, stop=True),
                        r=["c:sel", f"ls.{hb}", "psP1"], w=["psP1"])
                    tr.op("ACT", lambda e, _hsl=hsl: e.activation(out=lhat_s[:, _hsl],
                                                                  in_=psP1[0:8, 0:256],
                                                                  func=AF.Copy),
                          r=["psP1"], w=["lhat"])
                    tr.op("PE", lambda e, _hsl=hsl: e.matmul(
                        psP1[0:8, 0:256], lhsT=sel_s[:, 8:16], rhs=ls_s[:, _hsl],
                        start=True, stop=True),
                        r=["c:sel", f"ls.{hb}", "psP1"], w=["psP1"])
                    tr.op("ACT", lambda e, _hsl=hsl: e.activation(out=lgml_s[:, _hsl],
                                                                  in_=psP1[0:8, 0:256],
                                                                  func=AF.Copy),
                          r=["psP1"], w=["lgml"])
                for lc in range(4):
                    csl = slice(lc * 128, (lc + 1) * 128)
                    hb = lc // 2
                    tr.op("DVE", lambda e, _sl=csl: e.tensor_tensor_scan(
                        out=lbuf_s[:, _sl], data0=ls_s[0:8, _sl], data1=z8_s[:, :],
                        initial=0.0, op0=ALU.add, op1=ALU.add),
                        r=[f"ls.{hb}", "c:z8"], w=["lbuf"])
                tr.op("DVE", lambda e: e.tensor_add(
                    lhat_s[:, :], lhat_s[:, :], lbuf_s[:, :]),
                    r=["lbuf", "lhat"], w=["lhat"])
                tr.op("DVE", lambda e: e.tensor_sub(
                    lgml_s[:, :], lgml_s[:, :], lbuf_s[:, :]),
                    r=["lbuf", "lgml"], w=["lgml"])
                tr.op("ACT", lambda e, _gp=gp: e.activation(
                    out=ehat_s[_gp][:, :], in_=lhat_s[:, :], func=AF.Exp,
                    bias=cb_s[0:8, 3:4]),
                    r=["lhat", "c:cb"], w=[f"ehat{gp}"])
                for lc in range(4):
                    tr.op("DVE", lambda e, _lc=lc: e.tensor_scalar(
                        out=dg8_s[:, _lc * 8:(_lc + 1) * 8], in0=eyef_s[0:8, 0:8],
                        scalar1=lbuf_s[:, _lc * 128 + 127:_lc * 128 + 128], scalar2=None,
                        op0=ALU.mult),
                        r=["lbuf", "c:eyef"], w=["dg8"])

                def _hoist(e):
                    for lc in range(4):
                        e.transpose(psStat[:, lc * 16:lc * 16 + 8],
                                    lgml_s[:, lc * 128:(lc + 1) * 128], eyef_s[0:8, 0:8])
                    for lc in range(4):
                        i = e.matmul(psStat[:, lc * 16 + 8:lc * 16 + 16],
                                     lhsT=onesf_s[0:8, :],
                                     rhs=dg8_s[:, lc * 8:(lc + 1) * 8],
                                     start=True, stop=True)
                    return i
                tr.op("PE", _hoist, r=["lgml", "dg8", "c:eyef", "c:onesf"],
                      w=["psStat", "st:hoist", "st:mrep"])
                tr.op("ACT", lambda e, _gp=gp: e.activation(
                    out=tmg_s[_gp][:, :], in_=psStat[:, 0:64], func=AF.Copy),
                    r=["st:hoist"], w=[f"tmg{gp}"])

            wp_ctr = [0]

            def proj(g):
                gp = g % 2
                gsl = slice(g * 512, (g + 1) * 512)
                for pj, wsrc in (("v", wv16), ("k", wk16), ("q", wq16)):
                    if g == 0 and pj == "v":
                        tr.dma("POOLQ", "d_wq16",
                               lambda e: e.dma_start(out=wq16, in_=wqT), w=["wq16"])
                    sl0 = wp_ctr[0] % 3
                    sl1 = (wp_ctr[0] + 1) % 3
                    wp_ctr[0] += 2
                    for hf, wsl in ((0, sl0), (1, sl1)):
                        tr.dma("SYNCQ", f"d_wp{wsl}",
                               lambda e, _w=wsrc, _hf=hf, _wsl=wsl: e.dma_start(
                                   out=wpb[_wsl][:, :, :],
                                   in_=_w.rearrange("(c p) e -> p c e", p=128)
                                   [:, _hf * 4:(_hf + 1) * 4, :]),
                               r=[f"w{pj}16"], w=[f"wpb{wsl}"])
                    if pj == "v":
                        for tc4 in range(4):
                            cg = g * 4 + tc4
                            slot = cg % 8
                            for eh in range(2):
                                pp, pk = (psP0, "psP0") if eh == 0 else (psP1, "psP1")

                                def _vg(e, _tc=tc4, _eh=eh, _pp=pp, _gp=gp,
                                        _s0=sl0, _s1=sl1):
                                    for dc in range(DC):
                                        i = e.matmul(
                                            _pp[:, :],
                                            lhsT=xhat[_gp][:, dc, _tc * 128:(_tc + 1) * 128],
                                            rhs=wpb[_s0 if dc < 4 else _s1][:, dc % 4,
                                                             _eh * 512:(_eh + 1) * 512],
                                            start=(dc == 0), stop=(dc == DC - 1))
                                    return i
                                tr.op("PE", _vg,
                                      r=[f"xhat{gp}.{dc}" for dc in range(DC)]
                                      + [f"wpb{sl0}", f"wpb{sl1}", pk], w=[pk])
                                tr.op("ACT", lambda e, _s=slot, _eh=eh, _pp=pp: e.activation(
                                    out=vb[:, _s, _eh * 512:(_eh + 1) * 512], in_=_pp[:, :],
                                    func=AF.Copy),
                                    r=[pk], w=[f"vb.{slot}"])
                    else:
                        dst = kb if pj == "k" else qb
                        dkey = pj + "b"
                        for h in range(NH):
                            hp = h % 2
                            pp, pk = (psP0, "psP0") if hp == 0 else (psP1, "psP1")

                            def _qg(e, _h=h, _pp=pp, _gp=gp, _s0=sl0, _s1=sl1):
                                for dc in range(DC):
                                    i = e.matmul(
                                        _pp[:, :],
                                        lhsT=wpb[_s0 if dc < 4 else _s1][:, dc % 4,
                                                                         _h * 128:(_h + 1) * 128],
                                        rhs=xhat[_gp][:, dc, :],
                                        start=(dc == 0), stop=(dc == DC - 1))
                                return i
                            tr.op("PE", _qg,
                                  r=[f"xhat{gp}.{dc}" for dc in range(DC)]
                                  + [f"wpb{sl0}", f"wpb{sl1}", pk], w=[pk])
                            tr.op("ACT", lambda e, _pp=pp, _hp=hp: e.activation(
                                out=ropeA[_hp][:, :], in_=_pp[:, :], func=AF.Copy),
                                r=[pk], w=[f"ropeA{hp}"])
                            tr.op("PE", lambda e, _hp=hp: e.matmul(
                                psStat[:, :], lhsT=rotT_s[:, :], rhs=ropeA[_hp][:, :],
                                start=True, stop=True),
                                r=["c:rot", f"ropeA{hp}"],
                                w=["psStat", "st:mrep", "st:hoist"])
                            tr.op("DVE", lambda e, _hp=hp, _gp=gp: e.tensor_mul(
                                tmp1[_hp][:, :], ropeA[_hp][:, :], ct_s[_gp][:, :]),
                                r=[f"ropeA{hp}", f"ct{gp}"], w=[f"tmp1{hp}"])
                            tr.op("DVE", lambda e, _hp=hp, _gp=gp: e.tensor_mul(
                                tmp2[_hp][:, :], psStat[:, :], st_s[_gp][:, :]),
                                r=["psStat", f"st{gp}"], w=[f"tmp2{hp}"])
                            tr.op("DVE", lambda e, _d=dst, _h=h, _gp=gp, _hp=hp: e.tensor_add(
                                _d[:, _h, _gp * 512:(_gp + 1) * 512],
                                tmp1[_hp][:, :], tmp2[_hp][:, :]),
                                r=[f"tmp1{hp}", f"tmp2{hp}"], w=[f"{dkey}.{h}.g{g}"])

            def scanwout(g):
                gp = g % 2
                for lc in range(4):
                    cg = g * 4 + lc
                    cp = cg % 2
                    slot = cg % 8
                    csl = slice(lc * 128, (lc + 1) * 128)
                    rsl = slice(gp * 512 + lc * 128, gp * 512 + (lc + 1) * 128)
                    tr.op("ACT", lambda e, _gp=gp, _lc=lc: e.activation(
                        out=egml_s[:, :], in_=tmg_s[_gp][:, _lc * 16:_lc * 16 + 8],
                        func=AF.Exp),
                        r=[f"tmg{gp}", "uegate"], w=["egml"])
                    tr.op("DVE", lambda e, _gp=gp, _lc=lc: e.tensor_add(
                        ue8_s[:, :], tmg_s[_gp][:, _lc * 16:_lc * 16 + 8],
                        tmg_s[_gp][:, _lc * 16 + 8:_lc * 16 + 16]),
                        r=[f"tmg{gp}", "uegate"], w=["ue8"])
                    tr.op("ACT", lambda e: e.activation(out=ue8_s[:, :], in_=ue8_s[:, :],
                                                        func=AF.Exp),
                          r=["ue8"], w=["ue8"])
                    tr.op("ACT", lambda e, _gp=gp, _lc=lc: e.activation(
                        out=ac8_s[:, :], in_=tmg_s[_gp][:, _lc * 16 + 8:_lc * 16 + 16],
                        func=AF.Exp),
                        r=[f"tmg{gp}"], w=["ac8"])
                    tr.op("DVE", lambda e, _gp=gp, _csl=csl: e.tensor_mul(
                        em8_s[:, :].rearrange("p (h t) -> p h t", h=NH),
                        ehat_s[_gp][:, _csl].rearrange("p (o t) -> p o t", o=1)
                        .broadcast_to([8, NH, 128]),
                        hmask_s[:, :].rearrange("p (h t) -> p h t", h=NH)),
                        r=[f"ehat{gp}", "c:hmask"], w=["em8"])

                    def _bc(e):
                        for bk in range(2):
                            i = e.matmul(psE[:, bk * 512:(bk + 1) * 512],
                                         lhsT=onesb_s[0:8, :],
                                         rhs=em8_s[:, bk * 512:(bk + 1) * 512],
                                         start=True, stop=True)
                        return i
                    tr.op("PE", _bc, r=["em8", "c:onesb", "psE"], w=["psE"])
                    tr.op("DVE", lambda e, _cp=cp, _rsl=rsl: e.tensor_mul(
                        qtl[_cp][:, :, :],
                        qb[:, :, _rsl],
                        psE[:, :].rearrange("p (h t) -> p h t", h=NH)),
                        r=[f"qb.{h}.g{g}" for h in range(NH)] + ["psE"],
                        w=[f"qtl{cp}"])

                    def _ska(e, _cp=cp, _rsl=rsl):
                        for h in range(NH):
                            e.transpose(psXK[:, h * 128:(h + 1) * 128],
                                        kb[:, h, _rsl], eyeb_s[:, :])
                            i = e.matmul(psA[:, h * 128:(h + 1) * 128],
                                         lhsT=kb[:, h, _rsl], rhs=qtl[_cp][:, h, :],
                                         start=True, stop=True)
                        return i
                    tr.op("PE", _ska,
                          r=[f"kb.{h}.g{g}" for h in range(NH)]
                          + [f"qtl{cp}", "c:eyeb", "psXK", "psA"],
                          w=["psXK", "psA"])
                    tr.op("DVE", lambda e: e.tensor_mul(
                        pbt_s[:, :, :],
                        psA[:, :].rearrange("p (h t) -> p h t", h=NH),
                        egml_s[:, :].rearrange("p (h o) -> p h o", o=1)
                        .broadcast_to([128, NH, 128])),
                        r=["psA", "egml"], w=["pbt"])
                    tr.op("DVE", lambda e: e.tensor_mul(
                        pbt_s[:, :, :].rearrange("p h t -> p (h t)"),
                        pbt_s[:, :, :].rearrange("p h t -> p (h t)"),
                        mask8_s[:, :]),
                        r=["pbt", "c:mask8"], w=["pbt"])
                    tr.op("ACT", lambda e: e.activation(
                        out=ktm_s[:, :, :].rearrange("p h t -> p (h t)"), in_=psXK[:, :],
                        func=AF.Copy),
                        r=["psXK"], w=["ktm"])
                    if cg < NCH - 1:
                        def _vtg(e, _s=slot):
                            for h in range(NH):
                                i = e.tensor_scalar(
                                    out=vtl_s[:, h, :],
                                    in0=vb[:, _s, h * 128:(h + 1) * 128],
                                    scalar1=ue8_s[:, h:h + 1], scalar2=None, op0=ALU.mult)
                            return i
                        tr.op("DVE", _vtg, r=[f"vb.{slot}", "ue8"], w=["vtl", "uegate"])

                        def _kvg(e):
                            for h in range(NH):
                                i = e.matmul(psA[:, h * 128:(h + 1) * 128],
                                             lhsT=ktm_s[:, h, :], rhs=vtl_s[:, h, :],
                                             start=True, stop=True)
                            return i
                        tr.op("PE", _kvg, r=["ktm", "vtl", "pbt", "psA"],
                              w=["psA"])
                    if cg > 0:
                        def _yinter(e, _cp=cp):
                            for h in range(NH):
                                i = e.matmul(psE[:, h * 128:(h + 1) * 128],
                                             lhsT=mbb_s[:, h, :], rhs=qtl[_cp][:, h, :],
                                             start=True, stop=True)
                            return i
                        tr.op("PE", _yinter, r=["mbb", f"qtl{cp}", "psE"], w=["psE"])
                    if cg < NCH - 1:
                        if cg == 0:
                            for bk in range(2):
                                tr.op("DVE", lambda e, _b=bk: e.tensor_copy(
                                    out=msb_s[:, 4 * _b:4 * (_b + 1), :],
                                    in_=psA[:, _b * 512:(_b + 1) * 512]
                                    .rearrange("p (h t) -> p h t", h=4)),
                                    r=["psA"], w=["msb"])
                        else:
                            def _mdec(e):
                                for h in range(NH):
                                    i = e.tensor_scalar(
                                        out=msb_s[:, h, :], in0=msb_s[:, h, :],
                                        scalar1=ac8_s[:, h:h + 1], scalar2=None,
                                        op0=ALU.mult)
                                return i
                            tr.op("DVE", _mdec, r=["msb", "ac8"], w=["msb"])
                            for bk in range(2):
                                tr.op("DVE", lambda e, _b=bk: e.tensor_add(
                                    msb_s[:, 4 * _b:4 * (_b + 1), :],
                                    msb_s[:, 4 * _b:4 * (_b + 1), :],
                                    psA[:, _b * 512:(_b + 1) * 512]
                                    .rearrange("p (h t) -> p h t", h=4)),
                                    r=["msb", "psA"], w=["msb"])
                        tr.op("ACT", lambda e: e.activation(
                            out=mbb_s[:, :, :].rearrange("p h t -> p (h t)"),
                            in_=msb_s[:, :, :].rearrange("p h t -> p (h t)"), func=AF.Copy),
                            r=["msb"], w=["mbb"])
                    def _yintra(e, _s=slot):
                        for h in range(NH):
                            i = e.matmul(psA[:, h * 128:(h + 1) * 128],
                                         lhsT=vb[:, _s, h * 128:(h + 1) * 128],
                                         rhs=pbt_s[:, h, :],
                                         start=True, stop=True)
                        return i
                    tr.op("PE", _yintra, r=[f"vb.{slot}", "pbt", "psA"], w=["psA"])
                    if cg > 0:
                        for bk in range(2):
                            tr.op("DVE", lambda e, _lc=lc, _b=bk: e.tensor_copy(
                                out=yring[:, 4 * _b:4 * (_b + 1), _lc * 128:(_lc + 1) * 128],
                                in_=psE[:, _b * 512:(_b + 1) * 512]
                                .rearrange("p (h t) -> p h t", h=4)),
                                r=["psE"], w=[f"yring.{lc}"])
                        for bk in range(2):
                            tr.op("DVE", lambda e, _lc=lc, _b=bk: e.tensor_add(
                                yring[:, 4 * _b:4 * (_b + 1), _lc * 128:(_lc + 1) * 128],
                                yring[:, 4 * _b:4 * (_b + 1), _lc * 128:(_lc + 1) * 128],
                                psA[:, _b * 512:(_b + 1) * 512]
                                .rearrange("p (h t) -> p h t", h=4)),
                                r=["psA", f"yring.{lc}"], w=[f"yring.{lc}"])
                    else:
                        for bk in range(2):
                            tr.op("DVE", lambda e, _lc=lc, _b=bk: e.tensor_copy(
                                out=yring[:, 4 * _b:4 * (_b + 1), _lc * 128:(_lc + 1) * 128],
                                in_=psA[:, _b * 512:(_b + 1) * 512]
                                .rearrange("p (h t) -> p h t", h=4)),
                                r=["psA"], w=[f"yring.{lc}"])
                for wh in range(2):
                    for dcp in range(DC):
                        pp, pk = (psP0, "psP0") if dcp % 2 == 0 else (psP1, "psP1")

                        def _wog(e, _dcp=dcp, _pp=pp, _wh=wh):
                            for h in range(NH):
                                i = e.matmul(
                                    _pp[:, 0:256],
                                    lhsT=woutb[:, h, _dcp * 128:(_dcp + 1) * 128],
                                    rhs=yring[:, h, _wh * 256:(_wh + 1) * 256],
                                    start=(h == 0), stop=(h == NH - 1))
                            return i
                        tr.op("PE", _wog,
                              r=["woutb"] + [f"yring.{i}" for i in range(4)] + [pk], w=[pk])
                        tr.op("DVE", lambda e, _dcp=dcp, _pp=pp: e.tensor_scalar(
                            out=aosl[:, _dcp, :], in0=_pp[:, 0:256],
                            scalar1=bout_s[:, _dcp:_dcp + 1], scalar2=None, op0=ALU.add),
                            r=[pk, "c:bout"], w=["aosl"])
                    tr.dma("POOLQ", "d_ao",
                           lambda e, _g=g, _wh=wh: e.dma_start(
                               out=x1h.rearrange("(c p) t -> p c t", p=128)
                               [:, :, _g * 512 + _wh * 256:_g * 512 + (_wh + 1) * 256],
                               in_=aosl[:, :, :], accum_op=ALU.add),
                           r=["aosl", f"x1h.{g}"], w=[f"x1h.{g}"])

            frontS(0)
            if upto < 2:
                raise _Done()
            for g in range(NG):
                proj(g)
                if upto < 3:
                    raise _Done()
                frontG(g)
                if g == 0:
                    tr.dma("ACTQ", "d_x1init", lambda e: e.dma_start(out=x1h, in_=xTf),
                           w=[f"x1h.{gg}" for gg in range(NG)])
                if g + 1 < NG:
                    frontS(g + 1)
                if g > 0:
                    scanwout(g - 1)
            scanwout(NG - 1)

        tr.barrier()
        with ExitStack() as ph6:
            _e6 = ph6.enter_context
            wib = _e6(nc.sbuf_tensor("wibuf", [128, DC, FF], BF16))
            wob = _e6(nc.sbuf_tensor("wobuf", [128, FC, D], BF16))
            bcat = _e6(nc.sbuf_tensor("bcat", [32, FF], BF16))
            gab = _e6(nc.sbuf_tensor("gab", [128, DC, 8], BF16))
            aab = _e6(nc.sbuf_tensor("aab", [128, DC, 32], BF16))
            x1sl = _e6(nc.sbuf_tensor("x1sl", [128, DC, 512], BF16))
            fsc = _e6(nc.sbuf_tensor("fsc", [128, 512], BF16))
            xnb = [_e6(nc.sbuf_tensor(f"xnb{i}", [128, DC, 512], BF16)) for i in range(2)]
            inter = _e6(nc.sbuf_tensor("inter", [128, FC, 512], BF16))
            x2sl = [_e6(nc.sbuf_tensor("x2sl0", [128, 512], F32))] * 2
            sqscr = _e6(nc.sbuf_tensor("sqscr", [128, 512], BF16))
            m2bc = _e6(nc.sbuf_tensor("m2bc", [128, 512], F32))
            v2bc = _e6(nc.sbuf_tensor("v2bc", [128, 512], F32))
            rp8 = _e6(nc.sbuf_tensor("rp8", [8, 512], F32))
            pp32 = _e6(nc.sbuf_tensor("pp32", [32, 512], BF16))
            tm8 = _e6(nc.sbuf_tensor("tm8", [128, 32], F32))
            mx8 = _e6(nc.sbuf_tensor("mx8", [128, 8], F32))
            ix8 = _e6(nc.sbuf_tensor("ix8", [128, 8], U32))
            ixf = _e6(nc.sbuf_tensor("ixf", [128, 4], F32))
            m8t = _e6(nc.sbuf_tensor("m8t", [128, 32], F32))
            m8f = _e6(nc.sbuf_tensor("m8f", [8, 512], BF16))
            psS1 = _e6(nc.psum_tensor("psS1", [128, 512], F32))
            psS2 = _e6(nc.psum_tensor("psS2", [128, 512], F32))
            psR8 = _e6(nc.psum_tensor("psR8", [128, 512], F32))
            psP32 = _e6(nc.psum_tensor("psP32", [32, 512], F32))
            psF0 = _e6(nc.psum_tensor("psF0", [128, 512], F32))
            psF1 = _e6(nc.psum_tensor("psF1", [128, 512], F32))
            psO0 = _e6(nc.psum_tensor("psO0", [128, 512], F32))
            psO1 = _e6(nc.psum_tensor("psO1", [128, 512], F32))
            if upto < 5:
                raise _Done()
            tr.dma("POOLQ", "d_ga",
                   lambda e: e.dma_start(out=gab[:, :, :],
                                         in_=gw8T.rearrange("(c p) e -> p c e", p=128)),
                   w=["gab"])
            tr.dma("POOLQ", "d_aa",
                   lambda e: e.dma_start(out=aab[:, :, :],
                                         in_=acatT.rearrange("(c p) e -> p c e", p=128)),
                   w=["aab"])
            tr.dma("POOLQ", "d_bcat", lambda e: e.dma_start(out=bcat[:, :], in_=bcatT),
                   w=["bcat"])

            def load_ffn_weights():
                for i in range(4):
                    tr.dma("POOLQ", f"d_wi{i}",
                           lambda e, _i=i: e.dma_start(
                               out=wib[:, :, _i * 1024:(_i + 1) * 1024],
                               in_=wiT.rearrange("(c p) f -> p c f", p=128)
                               [:, :, _i * 1024:(_i + 1) * 1024]),
                           w=[f"wib.{i}"])
                for i in range(4):
                    tr.dma("POOLQ", f"d_wo{i}",
                           lambda e, _i=i: e.dma_start(
                               out=wob[:, 8 * _i:8 * _i + 8, :],
                               in_=woT.rearrange("(c p) f -> p c f", p=128)[:, 8 * _i:8 * _i + 8, :]),
                           w=[f"wob.{i}"])

            def frontA(g):
                bb = g % 2
                tr.dma("POOLQ", "d_x1sl",
                       lambda e, _g=g: e.dma_start(
                           out=x1sl[:, :, :],
                           in_=x1h.rearrange("(c p) t -> p c t", p=128)
                           [:, :, _g * 512:(_g + 1) * 512]),
                       r=[f"x1h.{g}"], w=["x1sl"])
                tr.dma("SYNCQ", f"d_oinit{g}",
                       lambda e, _g=g: e.dma_start(
                           out=outT[:, _g * 512:(_g + 1) * 512],
                           in_=x1h[:, _g * 512:(_g + 1) * 512]),
                       r=[f"x1h.{g}"], w=[f"outT.{g}"])

                def _s1g(e):
                    for dc in range(DC):
                        i = e.matmul(psS1[:, :], lhsT=onesb_s[:, :], rhs=x1sl[:, dc, :],
                                     start=(dc == 0), stop=(dc == DC - 1))
                    return i
                tr.op("PE", _s1g, r=["c:onesb", "x1sl", "psS1"], w=["psS1"])
                for dc in range(DC):
                    tr.op("ACT", lambda e, _dc=dc: e.activation(out=sqscr[:, :],
                                                                in_=x1sl[:, _dc, :],
                                                                func=AF.Square),
                          r=["x1sl"], w=["sqscr"])
                    tr.op("PE", lambda e, _dc=dc: e.matmul(
                        psS2[:, :], lhsT=onesb_s[:, :], rhs=sqscr[:, :],
                        start=(_dc == 0), stop=(_dc == DC - 1)),
                        r=["c:onesb", "sqscr", "psS2"], w=["psS2"])
                tr.op("ACT", lambda e: e.activation(out=m2bc[:, :], in_=psS1[:, :],
                                                    func=AF.Copy, scale=1.0 / D),
                      r=["psS1"], w=["m2bc"])
                tr.op("ACT", lambda e: e.activation(out=v2bc[:, :], in_=psS2[:, :],
                                                    func=AF.Copy, scale=1.0 / D),
                      r=["psS2"], w=["v2bc"])
                tr.op("DVE", lambda e: e.tensor_mul(fsc[:, :], m2bc[:, :], m2bc[:, :]),
                      r=["m2bc"], w=["fsc"])
                tr.op("DVE", lambda e: e.tensor_sub(v2bc[:, :], v2bc[:, :], fsc[:, :]),
                      r=["v2bc", "fsc"], w=["v2bc"])
                tr.op("ACT", lambda e: e.activation(out=v2bc[:, :], in_=v2bc[:, :],
                                                    func=AF.Ln, bias=cb_s[:, 2:3]),
                      r=["v2bc", "c:cb"], w=["v2bc"])
                tr.op("ACT", lambda e: e.activation(out=v2bc[:, :], in_=v2bc[:, :],
                                                    func=AF.Exp, scale=-0.5),
                      r=["v2bc"], w=["v2bc"])
                for dc in range(DC):
                    tr.op("DVE", lambda e, _dc=dc: e.tensor_sub(fsc[:, :], x1sl[:, _dc, :],
                                                                m2bc[:, :]),
                          r=["x1sl", "m2bc"], w=["fsc"])
                    tr.op("DVE", lambda e, _dc=dc, _bb=bb: e.tensor_mul(
                        xnb[_bb][:, _dc, :], fsc[:, :], v2bc[:, :]),
                        r=["fsc", "v2bc"], w=[f"xnb{bb}"])

            def frontB(g):
                bb = g % 2

                def _rg(e, _bb=bb):
                    for dc in range(DC):
                        e.matmul(psR8[0:8, :], lhsT=gab[:, dc, :], rhs=xnb[_bb][:, dc, :],
                                 start=(dc == 0), stop=(dc == DC - 1))
                        i = e.matmul(psP32[:, :], lhsT=aab[:, dc, :], rhs=xnb[_bb][:, dc, :],
                                     start=(dc == 0), stop=(dc == DC - 1))
                    return i
                tr.op("PE", _rg, r=["gab", "aab", f"xnb{bb}", "psR8", "psP32"],
                      w=["psR8", "psP32"])
                tr.op("DVE", lambda e: e.tensor_scalar(out=rp8[:, :], in0=psR8[0:8, :],
                                                       scalar1=g8_s[:, :], scalar2=None,
                                                       op0=ALU.add),
                      r=["psR8", "c:g8"], w=["rp8"])
                tr.op("ACT", lambda e: e.activation(out=pp32[:, :], in_=psP32[:, :],
                                                    func=AF.Copy),
                      r=["psP32"], w=["pp32"])
                for tc4 in range(4):
                    tr.op("PE", lambda e, _tc=tc4: e.transpose(
                        psS1[:, _tc * 8:(_tc + 1) * 8], rp8[:, _tc * 128:(_tc + 1) * 128],
                        eyef_s[0:8, 0:8]),
                        r=["rp8", "c:eyef", "psS1"], w=["psS1"])
                pe_settle(psS1[32:34, 40:42], ["psS1", "rp8"])
                tr.op("ACT", lambda e: e.activation(
                    out=tm8[:, :], in_=psS1[:, 0:32], func=AF.Copy),
                    r=["psS1"], w=["tm8"])
                for tc4 in range(4):
                    tr.op("DVE", lambda e, _tc=tc4: e.max(
                        out=mx8[:, :], in_=tm8[:, _tc * 8:(_tc + 1) * 8]),
                        r=["tm8"], w=["mx8"])
                    tr.op("DVE", lambda e, _tc=tc4: e.max_index(
                        out=ix8[:, :], in_max=mx8[:, :],
                        in_values=tm8[:, _tc * 8:(_tc + 1) * 8]),
                        r=["mx8", "tm8"], w=["ix8"])
                    tr.op("DVE", lambda e, _tc=tc4: e.tensor_copy(
                        out=ixf[:, _tc:_tc + 1], in_=ix8[:, 0:1]),
                        r=["ix8"], w=["ixf"])
                    tr.op("DVE", lambda e, _tc=tc4: e.tensor_scalar(
                        out=m8t[:, _tc * 8:(_tc + 1) * 8], in0=iota8_s[:, :],
                        scalar1=ixf[:, _tc:_tc + 1], scalar2=None, op0=ALU.is_equal),
                        r=["c:iota", "ixf"], w=["m8t"])
                for tc4 in range(4):
                    tr.op("PE", lambda e, _tc=tc4: e.transpose(
                        psR8[0:8, _tc * 128:(_tc + 1) * 128], m8t[:, _tc * 8:(_tc + 1) * 8],
                        eyef_s[:, :]),
                        r=["m8t", "c:eyef", "psR8"], w=["psR8"])
                pe_settle(psR8[32:34, 40:42], ["psR8", "m8t"])
                tr.op("ACT", lambda e: e.activation(
                    out=m8f[:, :], in_=psR8[0:8, :], func=AF.Copy),
                    r=["psR8"], w=["m8f"])
                tr.op("PE", lambda e: e.matmul(psP32[:, :], lhsT=expand_s[:, :], rhs=m8f[:, :],
                                               start=True, stop=True),
                      r=["c:exp", "m8f", "psP32"], w=["psP32"])
                tr.op("DVE", lambda e: e.tensor_mul(pp32[:, :], pp32[:, :], psP32[:, :]),
                      r=["pp32", "psP32"], w=["pp32"])

            def body_wi(g):
                bb = g % 2
                for fc in range(FC):
                    pp, pk = (psF0, "psF0") if fc % 2 == 0 else (psF1, "psF1")

                    def _wig(e, _fc=fc, _pp=pp, _bb=bb):
                        for dc in range(DC):
                            e.matmul(_pp[:, :],
                                     lhsT=wib[:, dc, _fc * 128:(_fc + 1) * 128],
                                     rhs=xnb[_bb][:, dc, :], start=(dc == 0), stop=False)
                        return e.matmul(_pp[:, :],
                                        lhsT=bcat[:, _fc * 128:(_fc + 1) * 128],
                                        rhs=pp32[:, :], start=False, stop=True)
                    tr.op("PE", _wig,
                          r=[f"wib.{fc // 8}", f"xnb{bb}", "bcat", "pp32", pk],
                          w=[pk])
                    tr.op("ACT", lambda e, _fc=fc, _pp=pp: e.activation(
                        out=inter[:, _fc, :], in_=_pp[:, :], func=AF.Relu,
                        bias=wib_s[:, _fc:_fc + 1]),
                        r=[pk, "c:wib"], w=[f"inter.{fc}"])

            def body_wo(g, d0, d1):
                for dcp in range(d0, d1):
                    pp, pk = (psO0, "psO0") if dcp % 2 == 0 else (psO1, "psO1")

                    def _wog2(e, _dcp=dcp, _pp=pp):
                        for fc in range(FC):
                            i = e.matmul(_pp[:, :],
                                         lhsT=wob[:, fc, _dcp * 128:(_dcp + 1) * 128],
                                         rhs=inter[:, fc, :], start=(fc == 0),
                                         stop=(fc == FC - 1))
                        return i
                    tr.op("PE", _wog2,
                          r=[f"wob.{i}" for i in range(4)]
                          + [f"inter.{fc}" for fc in range(FC)] + [pk],
                          w=[pk])
                    tr.op("DVE", lambda e, _dcp=dcp, _pp=pp: e.tensor_scalar(
                        out=x2sl[0][:, :], in0=_pp[:, :], scalar1=wob_s[:, _dcp:_dcp + 1],
                        scalar2=None, op0=ALU.add),
                        r=[pk, "c:wob", "x2sl0"], w=["x2sl0"])
                    tr.dma("POOLQ", "d_out0",
                           lambda e, _g=g, _dcp=dcp: e.dma_start(
                               out=outT[_dcp * 128:(_dcp + 1) * 128,
                                        _g * 512:(_g + 1) * 512],
                               in_=x2sl[0][:, :], accum_op=ALU.add),
                           r=["x2sl0", f"outT.{g}"], w=[f"outT.{g}"])

            frontA(0)
            frontB(0)
            load_ffn_weights()
            for g in range(NG):
                body_wi(g)
                if g + 1 < NG:
                    frontA(g + 1)
                body_wo(g, 0, 4)
                if g + 1 < NG:
                    frontB(g + 1)
                body_wo(g, 4, 8)
        raise _Done()
    except _Done:
        with ExitStack() as c2:
            tr.final_wait("POOLQ", "d_out0")
            tr.build_blocks(nc, c2)
    return nc


_NC_CACHE = {}


def _get_nc():
    if "nc" not in _NC_CACHE:
        _NC_CACHE["nc"] = build()
    return _NC_CACHE["nc"]


def kernel(**inputs):
    x = np.asarray(inputs["L0_attn_x"])
    nc = _get_nc()
    shared = _shared_arrays(inputs)
    in_maps = []
    for c in range(8):
        m = dict(shared)
        xt = np.ascontiguousarray(x[c].T)
        m["xT"] = xt
        m["xTf"] = xt
        in_maps.append(m)
    res = run_bass_kernel_spmd(nc, in_maps, core_ids=list(range(8)))
    return np.stack([np.asarray(r["outT"]).T for r in res.results], axis=0)


def _shared_arrays(inputs):
    ii = {k: np.asarray(v) for k, v in inputs.items()}
    wgP = np.zeros((D, 128), np.float32)
    wgP[:, 0:8] = ii["L0_attn_Wbeta"].T
    wgP[:, 32:40] = ii["L0_attn_Wig"].T
    wgP[:, 64:72] = ii["L0_attn_Wog"].T
    b128 = np.zeros((128, 1), np.float32)
    b128[0:8, 0] = ii["L0_attn_bbeta"]
    b128[32:40, 0] = ii["L0_attn_big"]
    b128[64:72, 0] = ii["L0_attn_bog"]
    return {
        "wqT": np.ascontiguousarray(ii["L0_attn_Wq"].T),
        "wkT": np.ascontiguousarray(ii["L0_attn_Wk"].T),
        "wvT": np.ascontiguousarray(ii["L0_attn_Wv"].T),
        "woutT": np.ascontiguousarray(ii["L0_attn_Wout"].T),
        "wgP": wgP, "b128": b128,
        "nw8": np.ascontiguousarray(ii["L0_attn_norm_w"].reshape(DC, 128).T),
        "gw8T": np.ascontiguousarray(ii["L0_ffn_gate_W"].T),
        "acatT": np.ascontiguousarray(
            np.transpose(ii["L0_ffn_lora_A"], (2, 0, 1)).reshape(D, 32)),
        "g8": np.ascontiguousarray(ii["L0_ffn_gate_b"].reshape(8, 1)),
        "bcatT": np.ascontiguousarray(
            np.transpose(ii["L0_ffn_lora_B"], (0, 2, 1)).reshape(32, FF)),
        "wiT": np.ascontiguousarray(ii["L0_ffn_wi_W"].T),
        "woT": np.ascontiguousarray(ii["L0_ffn_wo_W"].T),
        "wib32": np.ascontiguousarray(ii["L0_ffn_wi_b"].reshape(FC, 128).T),
        "wob8": np.ascontiguousarray(ii["L0_ffn_wo_b"].reshape(DC, 128).T),
        "bout8": np.ascontiguousarray(ii["L0_attn_bout"].reshape(DC, 128).T),
    }


# version 22
# speedup vs baseline: 1.0413x; 1.0010x over previous
import numpy as np
from contextlib import ExitStack

import concourse.bass as bass
import concourse.mybir as mybir
from concourse.bass_utils import run_bass_kernel_spmd

F32 = mybir.dt.float32
F32R = mybir.dt.float32r
BF16 = mybir.dt.bfloat16
U32 = mybir.dt.uint32
AF = mybir.ActivationFunctionType
ALU = mybir.AluOpType
AX = mybir.AxisListType

T, D, NH, DH, FF = 2048, 1024, 8, 128, 4096
NCH, NG, DC, FC = 16, 4, 8, 32
SCALE = 1.0 / float(np.sqrt(DH))

EMAP = {"PE": "tensor", "DVE": "vector", "ACT": "scalar", "POOL": "gpsimd",
        "SYNCQ": "sync", "ACTQ": "scalar", "POOLQ": "gpsimd"}
ENGS = ("PE", "DVE", "ACT", "POOL")


class TR:

    def __init__(self):
        self.cnt = {}
        self.lastw = {}
        self.readers = {}
        self.waited = {}
        self.floor = {}
        self.ops = {k: [] for k in ("tensor", "vector", "scalar", "gpsimd", "sync")}
        self.sem_names = set(ENGS)

    def barrier(self):
        self.floor = dict(self.cnt)

    def _wl(self, stream, r, w):
        need = dict(self.floor)
        for k in r:
            lw = self.lastw.get(k)
            if lw:
                need[lw[0]] = max(need.get(lw[0], 0), lw[1])
        for k in w:
            lw = self.lastw.get(k)
            if lw:
                need[lw[0]] = max(need.get(lw[0], 0), lw[1])
            for s2, v2 in self.readers.get(k, {}).items():
                need[s2] = max(need.get(s2, 0), v2)
        out = []
        wt = self.waited.setdefault(stream, {})
        for s2 in sorted(need):
            if wt.get(s2, 0) < need[s2]:
                wt[s2] = need[s2]
                out.append((s2, need[s2]))
        return out

    def _commit(self, sem, val, r, w):
        for k in w:
            self.lastw[k] = (sem, val)
            self.readers[k] = {}
        for k in r:
            self.readers.setdefault(k, {})[sem] = val

    def op(self, eng, fn, r=(), w=()):
        wl = self._wl(eng, r, w)
        self.cnt[eng] = self.cnt.get(eng, 0) + 1
        val = self.cnt[eng]

        def emit(engobj, sems, _wl=wl, _fn=fn, _eng=eng):
            for s2, v2 in _wl:
                engobj.wait_ge(sems[s2], v2)
            _fn(engobj).then_inc(sems[_eng], 1)
        self.ops[EMAP[eng]].append(emit)
        self._commit(eng, val, r, w)

    def dma(self, q, dsem, fn, r=(), w=()):
        self.sem_names.add(dsem)
        wl = self._wl(q, r, w)
        self.cnt[dsem] = self.cnt.get(dsem, 0) + 16
        val = self.cnt[dsem]

        def emit(engobj, sems, _wl=wl, _fn=fn, _ds=dsem):
            for s2, v2 in _wl:
                engobj.wait_ge(sems[s2], v2)
            _fn(engobj).then_inc(sems[_ds], 16)
        self.ops[EMAP[q]].append(emit)
        self._commit(dsem, val, r, w)

    def final_wait(self, stream, sem):
        v = self.cnt.get(sem, 0)
        self.ops[EMAP[stream]].append(
            lambda e, s, _sm=sem, _v=v: e.wait_ge(s[_sm], _v) if _v else None)

    def build_blocks(self, nc, ctx):
        sems = {n: ctx.enter_context(nc.semaphore(f"s_{n}"))
                for n in sorted(self.sem_names)}
        with nc.Block() as block:
            @block.sync
            def _(eng):
                for emit in self.ops["sync"]:
                    emit(eng, sems)

            @block.tensor
            def _(eng):
                for emit in self.ops["tensor"]:
                    emit(eng, sems)

            @block.vector
            def _(eng):
                for emit in self.ops["vector"]:
                    emit(eng, sems)

            @block.scalar
            def _(eng):
                for emit in self.ops["scalar"]:
                    emit(eng, sems)

            @block.gpsimd
            def _(eng):
                for emit in self.ops["gpsimd"]:
                    emit(eng, sems)


def build(upto=9):
    import ml_dtypes
    nc = bass.Bass("TRN2", target_bir_lowering=False, debug=False)

    def din(name, shape, dt=F32):
        return nc.dram_tensor(name, shape, dt, kind="ExternalInput").ap()
    xT = din("xT", [D, T], F32R)
    xTf = din("xTf", [D, T])
    wqT = din("wqT", [D, D]); wkT = din("wkT", [D, D]); wvT = din("wvT", [D, D])
    woutT = din("woutT", [D, D])
    wgP = din("wgP", [D, 128])
    b128 = din("b128", [128, 1])
    nw8 = din("nw8", [128, DC])
    gw8T = din("gw8T", [D, 8])
    acatT = din("acatT", [D, 32])
    g8 = din("g8", [8, 1])
    bcatT = din("bcatT", [32, FF])
    wiT = din("wiT", [D, FF])
    woT = din("woT", [FF, D])
    wib32 = din("wib32", [128, FC])
    wob8 = din("wob8", [128, DC])
    bout8 = din("bout8", [128, DC])
    outT = nc.dram_tensor("outT", [D, T], F32, kind="ExternalOutput").ap()
    x1h = nc.dram_tensor("x1h", [D, T], F32, kind="Internal").ap()
    wv16 = nc.dram_tensor("wv16", [D, D], BF16, kind="Internal").ap()
    wk16 = nc.dram_tensor("wk16", [D, D], BF16, kind="Internal").ap()
    wq16 = nc.dram_tensor("wq16", [D, D], BF16, kind="Internal").ap()

    eye_f = nc.inline_tensor(np.eye(128, dtype=np.float32), "eye_f").ap()
    eye_b = nc.inline_tensor(np.eye(128, dtype=ml_dtypes.bfloat16), "eye_b").ap()
    ones_f = nc.inline_tensor(np.ones((128, 128), np.float32), "ones_f").ap()
    m01 = (np.arange(128)[:, None] <= np.arange(128)[None, :]).astype(np.float64)
    mask8_np = np.tile(m01, (1, 8)).astype(ml_dtypes.bfloat16)
    mask8_c = nc.inline_tensor(mask8_np, "mask8").ap()
    hm_np = np.zeros((8, 1024), np.float64)
    for h in range(8):
        hm_np[h, h * 128:(h + 1) * 128] = 1.0
    hmask_c = nc.inline_tensor(hm_np.astype(ml_dtypes.bfloat16), "hmask8").ap()
    iota8_c = nc.inline_tensor(
        np.broadcast_to(np.arange(8, dtype=np.float32), (128, 8)).copy(), "iota8").ap()
    exp_np = np.zeros((8, 32), ml_dtypes.bfloat16)
    for e in range(8):
        exp_np[e, e * 4:(e + 1) * 4] = 1.0
    expand_c = nc.inline_tensor(exp_np, "expand832").ap()
    rot = np.zeros((128, 128), np.float64)
    rot[np.arange(64), 64 + np.arange(64)] = -1.0
    rot[64 + np.arange(64), np.arange(64)] = 1.0
    rotT_c = nc.inline_tensor(rot.T.astype(ml_dtypes.bfloat16), "rotT").ap()
    sel = np.zeros((128, 16), np.float32)
    sel[64 + np.arange(8), np.arange(8)] = 1.0
    sel[32 + np.arange(8), 8 + np.arange(8)] = 1.0
    sel_c = nc.inline_tensor(sel, "sel16").ap()
    half = DH // 2
    inv_freq = 1.0 / (10000.0 ** (np.arange(half, dtype=np.float64) / half))
    frq = np.arange(T, dtype=np.float64)[None, :] * inv_freq[:, None]
    cs2 = np.concatenate([np.cos(frq), np.cos(frq)], axis=0).astype(ml_dtypes.bfloat16)
    sn2 = np.concatenate([np.sin(frq), np.sin(frq)], axis=0).astype(ml_dtypes.bfloat16)
    ctab_c = nc.inline_tensor(cs2, "ctab").ap()
    stab_c = nc.inline_tensor(sn2, "stab").ap()
    z8_c = nc.inline_tensor(np.zeros((8, 128), np.float32), "z8").ap()

    tr = TR()

    class _Done(Exception):
        pass

    def ldc(dst, src, key):
        tr.dma("SYNCQ", "dc_" + key.replace(":", ""),
               lambda e, _d=dst, _s=src: e.dma_start(out=_d, in_=_s), w=[key])

    def pe_settle(ps_ap, keys):
        tr.op("PE", lambda e, _p=ps_ap: e.matmul(
            _p, lhsT=onesf_s[0:1, 0:2], rhs=onesf_s[0:1, 0:2],
            start=True, stop=True), r=["c:onesf"], w=list(keys))

    ctx = ExitStack()
    try:
      with ctx:
        SB = lambda n, s, d=F32: ctx.enter_context(nc.sbuf_tensor(n, s, d))
        eyef_s = SB("eyef", [128, 128])
        eyeb_s = SB("eyeb", [128, 128], BF16)
        onesf_s = SB("onesf", [128, 128]); onesr_s = SB("onesr", [128, 128], F32R)
        onesb_s = SB("onesb", [128, 128], BF16)
        iota8_s = SB("iota8s", [128, 8])
        expand_s = SB("expand", [8, 32], BF16)
        nw_s = SB("nw", [128, DC]); b128_s = SB("b128s", [128, 1])
        g8_s = SB("g8s", [8, 1])
        wib_s = SB("wibc", [128, FC]); wob_s = SB("wobc", [128, DC]); bout_s = SB("boutc", [128, DC])
        cb_s = SB("cbias", [128, 4])
        ldc(eyef_s[:, :], eye_f, "c:eyef")
        ldc(eyeb_s[:, :], eye_b, "c:eyeb")
        ldc(onesf_s[:, :], ones_f, "c:onesf")
        ldc(iota8_s[:, :], iota8_c, "c:iota"); ldc(expand_s[:, :], expand_c, "c:exp")
        ldc(nw_s[:, :], nw8, "c:nw"); ldc(b128_s[:, :], b128, "c:b128")
        ldc(g8_s[:, :], g8, "c:g8"); ldc(wib_s[:, :], wib32, "c:wib")
        ldc(wob_s[:, :], wob8, "c:wob"); ldc(bout_s[:, :], bout8, "c:bout")
        for j, v in ((0, 1e-11), (1, 1.0), (2, 1e-5),
                     (3, float(np.log(SCALE)))):
            tr.op("DVE", lambda e, _j=j, _v=v: e.memset(cb_s[:, _j:_j + 1], _v), w=["c:cb"])
        tr.op("DVE", lambda e: e.tensor_copy(out=onesr_s[:, :], in_=onesf_s[:, :]),
              r=["c:onesf"], w=["c:onesr"])
        tr.op("DVE", lambda e: e.tensor_copy(out=onesb_s[:, :], in_=onesf_s[:, :]),
              r=["c:onesf"], w=["c:onesb"])

        with ExitStack() as attn:
            _a = attn.enter_context
            ct_s = [_a(nc.sbuf_tensor(f"cts{i}", [128, 512], BF16)) for i in range(2)]
            st_s = [_a(nc.sbuf_tensor(f"sts{i}", [128, 512], BF16)) for i in range(2)]
            rotT_s = _a(nc.sbuf_tensor("rotTs", [128, 128], BF16))
            mask8_s = _a(nc.sbuf_tensor("mask8s", [128, 1024], BF16))
            hmask_s = _a(nc.sbuf_tensor("hmasks", [8, 1024], BF16))
            sel_s = _a(nc.sbuf_tensor("sel16s", [128, 16], F32))
            dg8_s = _a(nc.sbuf_tensor("dg8s", [8, 32], F32))
            z8_s = _a(nc.sbuf_tensor("z8s", [8, 128], F32))
            xsl3 = [_a(nc.sbuf_tensor(f"xsl3{i}", [128, DC, 256], F32R)) for i in range(2)]
            xsq = _a(nc.sbuf_tensor("xsq", [128, DC, 256], BF16))
            xhat = [_a(nc.sbuf_tensor(f"xhat{i}", [128, DC, 512], BF16)) for i in range(2)]
            rrep_s = _a(nc.sbuf_tensor("rrep", [128, 512], F32))
            t1_s = _a(nc.sbuf_tensor("t1s", [128, 256], F32))
            mrow_s = _a(nc.sbuf_tensor("mrow", [1, 1536], F32))
            mrr_s = _a(nc.sbuf_tensor("mrr", [1, 1024], F32R))
            wg_s = _a(nc.sbuf_tensor("wg", [128, DC, 128], F32R))
            wsg_s = _a(nc.sbuf_tensor("wsg", [1, 128], F32R))
            colt_s = _a(nc.sbuf_tensor("colt", [128, 1], F32))
            zt_s = _a(nc.sbuf_tensor("zt", [128, 256], F32))
            sg1_s = _a(nc.sbuf_tensor("sg1", [128, 256], F32))
            ls_s = _a(nc.sbuf_tensor("ls", [128, 512], F32))
            lhat_s = _a(nc.sbuf_tensor("lhat", [8, 512], F32))
            lgml_s = _a(nc.sbuf_tensor("lgml", [8, 512], F32))
            lbuf_s = _a(nc.sbuf_tensor("lbuf", [8, 512], F32))
            ehat_s = [_a(nc.sbuf_tensor(f"ehat{i}", [8, 512], F32)) for i in range(2)]
            em8_s = _a(nc.sbuf_tensor("em8", [8, 1024], BF16))
            tmg_s = [_a(nc.sbuf_tensor(f"tmg{i}", [128, 64], F32)) for i in range(2)]
            egml_s = _a(nc.sbuf_tensor("egml", [128, 8], F32))
            ue8_s = _a(nc.sbuf_tensor("ue8", [128, 8], F32))
            ac8_s = _a(nc.sbuf_tensor("ac8", [128, 8], F32))
            wpb = [_a(nc.sbuf_tensor(f"wpb{i}", [128, 4, 1024], BF16)) for i in range(3)]
            qb = _a(nc.sbuf_tensor("qb", [128, NH, 1024], BF16))
            kb = _a(nc.sbuf_tensor("kb", [128, NH, 1024], BF16))
            vb = _a(nc.sbuf_tensor("vb", [128, 8, D], BF16))
            ropeA = [_a(nc.sbuf_tensor(f"ropeA{i}", [128, 512], BF16)) for i in range(2)]
            tmp1 = [_a(nc.sbuf_tensor(f"tmp1{i}", [128, 512], BF16)) for i in range(2)]
            tmp2 = [_a(nc.sbuf_tensor(f"tmp2{i}", [128, 512], BF16)) for i in range(2)]
            qtl = [_a(nc.sbuf_tensor(f"qtl{i}", [128, NH, 128], BF16)) for i in range(2)]
            pbt_s = _a(nc.sbuf_tensor("pbt", [128, NH, 128], BF16))
            vtl_s = _a(nc.sbuf_tensor("vtl", [128, NH, 128], BF16))
            ktm_s = _a(nc.sbuf_tensor("ktm", [128, NH, 128], BF16))
            msb_s = _a(nc.sbuf_tensor("msb", [128, NH, DH], F32))
            mbb_s = _a(nc.sbuf_tensor("mbb", [128, NH, DH], BF16))
            yring = _a(nc.sbuf_tensor("yring", [128, NH, 512], BF16))
            aosl = _a(nc.sbuf_tensor("aosl", [128, DC, 256], F32))
            woutb = _a(nc.sbuf_tensor("woutb", [128, NH, D], BF16))
            psP0 = _a(nc.psum_tensor("psP0", [128, 512], F32))
            psP1 = _a(nc.psum_tensor("psP1", [128, 512], F32))
            psStat = _a(nc.psum_tensor("psStat", [128, 512], F32))
            psE = _a(nc.psum_tensor("psE", [128, NH * 128], F32))
            psA = _a(nc.psum_tensor("psA", [128, NH * 128], F32))
            psXK = _a(nc.psum_tensor("psXK", [128, NH * 128], BF16))
            if upto < 1:
                raise _Done()

            ldc(rotT_s[:, :], rotT_c, "c:rot"); ldc(mask8_s[:, :], mask8_c, "c:mask8")
            ldc(hmask_s[:, :], hmask_c, "c:hmask"); ldc(sel_s[:, :], sel_c, "c:sel")
            ldc(z8_s[:, :], z8_c, "c:z8")

            tr.dma("POOLQ", "d_wg",
                   lambda e: e.dma_start(out=wg_s[:, :, :],
                                         in_=wgP.rearrange("(c p) e -> p c e", p=128)),
                   w=["wg"])
            tr.dma("POOLQ", "d_wv16", lambda e: e.dma_start(out=wv16, in_=wvT), w=["wv16"])
            tr.dma("POOLQ", "d_wk16", lambda e: e.dma_start(out=wk16, in_=wkT), w=["wk16"])

            def _fold(e):
                for dc in range(DC):
                    i = e.mul(out=wg_s[:, dc, :], in_=wg_s[:, dc, :],
                              mul=nw_s[:, dc:dc + 1])
                return i
            tr.op("ACT", _fold, r=["wg", "c:nw"], w=["wg"])

            def _wgcs(e):
                for dc in range(DC):
                    i = e.matmul(psE[:, 0:2], lhsT=wg_s[:, dc, :], rhs=onesr_s[:, 0:2],
                                 start=(dc == 0), stop=(dc == DC - 1))
                return i
            tr.op("PE", _wgcs, r=["wg", "c:onesr", "psE"], w=["psE"])
            tr.op("ACT", lambda e: e.activation(out=colt_s[:, :], in_=psE[:, 0:1],
                                                func=AF.Copy),
                  r=["psE"], w=["colt"])
            tr.op("PE", lambda e: e.transpose(psE[0:1, 0:128], colt_s[:, :], eyef_s[:, :]),
                  r=["colt", "c:eyef", "psE"], w=["psE"])
            pe_settle(psE[32:34, 200:202], ["psE"])
            tr.op("ACT", lambda e: e.activation(out=wsg_s[0:1, :], in_=psE[0:1, 0:128],
                                                func=AF.Copy, scale=-1.0),
                  r=["psE"], w=["wsg"])
            tr.dma("POOLQ", "d_wout",
                   lambda e: e.dma_start(out=woutb[:, :, :],
                                         in_=woutT.rearrange("(c p) e -> p c e", p=128)),
                   w=["woutb"])

            MR = {"mrow": 0, "msq": 512, "vrow": 1024}
            RR = {"mrowR": 0, "rrow": 512}

            def mrs(k):
                return mrow_s[0:1, MR[k]:MR[k] + 512]

            def rrs(k):
                return mrr_s[0:1, RR[k]:RR[k] + 512]

            def frontS(g):
                gp = g % 2
                gsl = slice(g * 512, (g + 1) * 512)
                tr.dma("SYNCQ", f"d_ct{gp}",
                       lambda e, _gp=gp, _gsl=gsl: e.dma_start(
                           out=ct_s[_gp][:, :], in_=ctab_c[:, _gsl]),
                       w=[f"ct{gp}"])
                tr.dma("SYNCQ", f"d_st{gp}",
                       lambda e, _gp=gp, _gsl=gsl: e.dma_start(
                           out=st_s[_gp][:, :], in_=stab_c[:, _gsl]),
                       w=[f"st{gp}"])
                for hb in range(2):
                    xg256 = slice(g * 512 + hb * 256, g * 512 + (hb + 1) * 256)
                    tr.dma("SYNCQ", f"d_xsl3{hb}",
                           lambda e, _hb=hb, _s=xg256: e.dma_start(
                               out=xsl3[_hb][:, :, :],
                               in_=xT.rearrange("(c p) t -> p c t", p=128)[:, :, _s]),
                           w=[f"xsl3{hb}"])
                    tr.op("ACT", lambda e, _hb=hb: e.activation(
                        out=xsq[:, :, :].rearrange("p c t -> p (c t)"),
                        in_=xsl3[_hb][:, :, :].rearrange("p c t -> p (c t)"),
                        func=AF.Square),
                        r=[f"xsl3{hb}"], w=["xsq"])

                    def _sums(e, _hb=hb):
                        for dc in range(DC):
                            e.matmul(psStat[0:1, _hb * 256:(_hb + 1) * 256],
                                     lhsT=onesr_s[:, 0:1], rhs=xsl3[_hb][:, dc, :],
                                     start=(dc == 0), stop=(dc == DC - 1))
                        for dc in range(DC):
                            i = e.matmul(psP1[0:1, _hb * 256:(_hb + 1) * 256],
                                         lhsT=onesb_s[:, 0:1], rhs=xsq[:, dc, :],
                                         start=(dc == 0), stop=(dc == DC - 1))
                        return i
                    tr.op("PE", _sums,
                          r=[f"xsl3{hb}", "xsq", "c:onesr", "c:onesb", "psStat", "psP1"],
                          w=["psStat", "st:rows", "psP1"])
                tr.op("ACT", lambda e: e.activation(out=rrs("mrowR"), in_=psStat[0:1, :],
                                                    func=AF.Copy, scale=1.0 / D),
                      r=["st:rows"], w=["r:mrowR"])
                tr.op("ACT", lambda e: e.activation(out=mrs("msq"), in_=rrs("mrowR"),
                                                    func=AF.Square),
                      r=["r:mrowR"], w=["r:msq"])
                tr.op("DVE", lambda e: e.scalar_tensor_tensor(
                    out=mrs("vrow"), in0=psP1[0:1, :], scalar=1.0 / D,
                    in1=mrs("msq"), op0=ALU.mult, op1=ALU.subtract),
                      r=["st:rows", "psP1", "r:msq"], w=["r:vrow"])
                tr.op("ACT", lambda e: e.activation(out=mrs("vrow"), in_=mrs("vrow"),
                                                    func=AF.Ln, scale=1.000001,
                                                    bias=cb_s[0:1, 0:1]),
                      r=["r:vrow", "c:cb"], w=["r:vrow"])
                tr.op("ACT", lambda e: e.activation(out=rrs("rrow"), in_=mrs("vrow"),
                                                    func=AF.Exp, scale=-0.5),
                      r=["r:vrow"], w=["r:rrow"])
                tr.op("PE", lambda e: e.matmul(psP0[:, :], lhsT=onesr_s[0:1, :],
                                               rhs=rrs("rrow"), start=True, stop=True),
                      r=["r:rrow", "c:onesr", "psP0"], w=["psP0"])
                tr.op("ACT", lambda e: e.activation(out=rrep_s[:, :], in_=psP0[:, :],
                                                    func=AF.Copy),
                      r=["psP0"], w=["rrep"])
                tr.op("PE", lambda e: e.matmul(psStat[:, :], lhsT=onesr_s[0:1, :],
                                               rhs=rrs("mrowR"), start=True, stop=True),
                      r=["r:mrowR", "c:onesr", "psStat", "st:rows"], w=["psStat", "st:mrep"])
                for hb in range(2):
                    hsl = slice(hb * 256, (hb + 1) * 256)
                    for dc in range(DC):
                        tr.op("DVE", lambda e, _hb=hb, _dc=dc, _hsl=hsl: e.tensor_sub(
                            t1_s[:, :], xsl3[_hb][:, _dc, :], psStat[:, _hsl]),
                            r=[f"xsl3{hb}", "st:mrep"], w=["t1"])
                        tr.op("DVE", lambda e, _hb=hb, _dc=dc, _hsl=hsl, _gp=gp:
                              e.scalar_tensor_tensor(
                                  out=xhat[_gp][:, _dc, _hsl], in0=t1_s[:, :],
                                  scalar=nw_s[:, _dc:_dc + 1], in1=rrep_s[:, _hsl],
                                  op0=ALU.mult, op1=ALU.mult),
                              r=["t1", "c:nw", "rrep"], w=[f"xhat{gp}.{dc}"])
            def frontG(g):
                gp = g % 2
                gsl = slice(g * 512, (g + 1) * 512)
                for hb in range(2):
                    hsl = slice(hb * 256, (hb + 1) * 256)

                    def _zg(e, _hb=hb):
                        for dc in range(DC):
                            e.matmul(psP0[:, 0:256], lhsT=wg_s[:, dc, :],
                                     rhs=xsl3[_hb][:, dc, :], start=(dc == 0), stop=False)
                        return e.matmul(psP0[:, 0:256], lhsT=wsg_s[0:1, :],
                                        rhs=mrr_s[0:1, _hb * 256:(_hb + 1) * 256],
                                        start=False, stop=True)
                    tr.op("PE", _zg, r=["wg", f"xsl3{hb}", "wsg", "r:mrowR", "psP0"],
                          w=["psP0"])
                    tr.op("DVE", lambda e, _hsl=hsl: e.tensor_mul(zt_s[:, :], psP0[:, 0:256],
                                                                  rrep_s[:, _hsl]),
                          r=["psP0", "rrep"], w=["zt"])
                    tr.op("DVE", lambda e: e.tensor_scalar(out=zt_s[:, :], in0=zt_s[:, :],
                                                           scalar1=b128_s[:, :], scalar2=None,
                                                           op0=ALU.add),
                          r=["zt", "c:b128"], w=["zt"])
                    tr.op("ACT", lambda e: e.activation(out=sg1_s[:, :], in_=zt_s[:, :],
                                                        func=AF.Abs),
                          r=["zt"], w=["sg1"])
                    tr.op("ACT", lambda e: e.activation(out=sg1_s[:, :], in_=sg1_s[:, :],
                                                        func=AF.Exp, scale=-1.0),
                          r=["sg1"], w=["sg1"])
                    tr.op("ACT", lambda e: e.activation(out=sg1_s[:, :], in_=sg1_s[:, :],
                                                        func=AF.Ln, bias=cb_s[:, 1:2]),
                          r=["sg1", "c:cb"], w=["sg1"])
                    tr.op("DVE", lambda e: e.tensor_scalar_min(zt_s[:, :], zt_s[:, :], 0.0),
                          r=["zt"], w=["zt"])
                    tr.op("DVE", lambda e, _hsl=hsl: e.tensor_sub(ls_s[:, _hsl], zt_s[:, :],
                                                                  sg1_s[:, :]),
                          r=["zt", "sg1"], w=[f"ls.{hb}"])
                    tr.op("PE", lambda e, _hsl=hsl: e.matmul(
                        psP1[0:8, 0:256], lhsT=sel_s[:, 0:8], rhs=ls_s[:, _hsl],
                        start=True, stop=True),
                        r=["c:sel", f"ls.{hb}", "psP1"], w=["psP1"])
                    tr.op("ACT", lambda e, _hsl=hsl: e.activation(out=lhat_s[:, _hsl],
                                                                  in_=psP1[0:8, 0:256],
                                                                  func=AF.Copy),
                          r=["psP1"], w=["lhat"])
                    tr.op("PE", lambda e, _hsl=hsl: e.matmul(
                        psP1[0:8, 0:256], lhsT=sel_s[:, 8:16], rhs=ls_s[:, _hsl],
                        start=True, stop=True),
                        r=["c:sel", f"ls.{hb}", "psP1"], w=["psP1"])
                    tr.op("ACT", lambda e, _hsl=hsl: e.activation(out=lgml_s[:, _hsl],
                                                                  in_=psP1[0:8, 0:256],
                                                                  func=AF.Copy),
                          r=["psP1"], w=["lgml"])
                for lc in range(4):
                    csl = slice(lc * 128, (lc + 1) * 128)
                    hb = lc // 2
                    tr.op("DVE", lambda e, _sl=csl: e.tensor_tensor_scan(
                        out=lbuf_s[:, _sl], data0=ls_s[0:8, _sl], data1=z8_s[:, :],
                        initial=0.0, op0=ALU.add, op1=ALU.add),
                        r=[f"ls.{hb}", "c:z8"], w=["lbuf"])
                tr.op("DVE", lambda e: e.tensor_add(
                    lhat_s[:, :], lhat_s[:, :], lbuf_s[:, :]),
                    r=["lbuf", "lhat"], w=["lhat"])
                tr.op("DVE", lambda e: e.tensor_sub(
                    lgml_s[:, :], lgml_s[:, :], lbuf_s[:, :]),
                    r=["lbuf", "lgml"], w=["lgml"])
                tr.op("ACT", lambda e, _gp=gp: e.activation(
                    out=ehat_s[_gp][:, :], in_=lhat_s[:, :], func=AF.Exp,
                    bias=cb_s[0:8, 3:4]),
                    r=["lhat", "c:cb"], w=[f"ehat{gp}"])
                for lc in range(4):
                    tr.op("DVE", lambda e, _lc=lc: e.tensor_scalar(
                        out=dg8_s[:, _lc * 8:(_lc + 1) * 8], in0=eyef_s[0:8, 0:8],
                        scalar1=lbuf_s[:, _lc * 128 + 127:_lc * 128 + 128], scalar2=None,
                        op0=ALU.mult),
                        r=["lbuf", "c:eyef"], w=["dg8"])

                def _hoist(e):
                    for lc in range(4):
                        e.transpose(psStat[:, lc * 16:lc * 16 + 8],
                                    lgml_s[:, lc * 128:(lc + 1) * 128], eyef_s[0:8, 0:8])
                    for lc in range(4):
                        i = e.matmul(psStat[:, lc * 16 + 8:lc * 16 + 16],
                                     lhsT=onesf_s[0:8, :],
                                     rhs=dg8_s[:, lc * 8:(lc + 1) * 8],
                                     start=True, stop=True)
                    return i
                tr.op("PE", _hoist, r=["lgml", "dg8", "c:eyef", "c:onesf"],
                      w=["psStat", "st:hoist", "st:mrep"])
                tr.op("ACT", lambda e, _gp=gp: e.activation(
                    out=tmg_s[_gp][:, :], in_=psStat[:, 0:64], func=AF.Copy),
                    r=["st:hoist"], w=[f"tmg{gp}"])

            wp_ctr = [0]

            def proj(g):
                gp = g % 2
                gsl = slice(g * 512, (g + 1) * 512)
                for pj, wsrc in (("v", wv16), ("k", wk16), ("q", wq16)):
                    if g == 0 and pj == "v":
                        tr.dma("POOLQ", "d_wq16",
                               lambda e: e.dma_start(out=wq16, in_=wqT), w=["wq16"])
                    sl0 = wp_ctr[0] % 3
                    sl1 = (wp_ctr[0] + 1) % 3
                    wp_ctr[0] += 2
                    for hf, wsl in ((0, sl0), (1, sl1)):
                        tr.dma("SYNCQ", f"d_wp{wsl}",
                               lambda e, _w=wsrc, _hf=hf, _wsl=wsl: e.dma_start(
                                   out=wpb[_wsl][:, :, :],
                                   in_=_w.rearrange("(c p) e -> p c e", p=128)
                                   [:, _hf * 4:(_hf + 1) * 4, :]),
                               r=[f"w{pj}16"], w=[f"wpb{wsl}"])
                    if pj == "v":
                        for tc4 in range(4):
                            cg = g * 4 + tc4
                            slot = cg % 8
                            for eh in range(2):
                                pp, pk = (psP0, "psP0") if eh == 0 else (psP1, "psP1")

                                def _vg(e, _tc=tc4, _eh=eh, _pp=pp, _gp=gp,
                                        _s0=sl0, _s1=sl1):
                                    for dc in range(DC):
                                        i = e.matmul(
                                            _pp[:, :],
                                            lhsT=xhat[_gp][:, dc, _tc * 128:(_tc + 1) * 128],
                                            rhs=wpb[_s0 if dc < 4 else _s1][:, dc % 4,
                                                             _eh * 512:(_eh + 1) * 512],
                                            start=(dc == 0), stop=(dc == DC - 1))
                                    return i
                                tr.op("PE", _vg,
                                      r=[f"xhat{gp}.{dc}" for dc in range(DC)]
                                      + [f"wpb{sl0}", f"wpb{sl1}", pk], w=[pk])
                                tr.op("ACT", lambda e, _s=slot, _eh=eh, _pp=pp: e.activation(
                                    out=vb[:, _s, _eh * 512:(_eh + 1) * 512], in_=_pp[:, :],
                                    func=AF.Copy),
                                    r=[pk], w=[f"vb.{slot}"])
                    else:
                        dst = kb if pj == "k" else qb
                        dkey = pj + "b"
                        for h in range(NH):
                            hp = h % 2
                            pp, pk = (psP0, "psP0") if hp == 0 else (psP1, "psP1")

                            def _qg(e, _h=h, _pp=pp, _gp=gp, _s0=sl0, _s1=sl1):
                                for dc in range(DC):
                                    i = e.matmul(
                                        _pp[:, :],
                                        lhsT=wpb[_s0 if dc < 4 else _s1][:, dc % 4,
                                                                         _h * 128:(_h + 1) * 128],
                                        rhs=xhat[_gp][:, dc, :],
                                        start=(dc == 0), stop=(dc == DC - 1))
                                return i
                            tr.op("PE", _qg,
                                  r=[f"xhat{gp}.{dc}" for dc in range(DC)]
                                  + [f"wpb{sl0}", f"wpb{sl1}", pk], w=[pk])
                            tr.op("ACT", lambda e, _pp=pp, _hp=hp: e.activation(
                                out=ropeA[_hp][:, :], in_=_pp[:, :], func=AF.Copy),
                                r=[pk], w=[f"ropeA{hp}"])
                            tr.op("PE", lambda e, _hp=hp: e.matmul(
                                psStat[:, :], lhsT=rotT_s[:, :], rhs=ropeA[_hp][:, :],
                                start=True, stop=True),
                                r=["c:rot", f"ropeA{hp}"],
                                w=["psStat", "st:mrep", "st:hoist"])
                            tr.op("DVE", lambda e, _hp=hp, _gp=gp: e.tensor_mul(
                                tmp1[_hp][:, :], ropeA[_hp][:, :], ct_s[_gp][:, :]),
                                r=[f"ropeA{hp}", f"ct{gp}"], w=[f"tmp1{hp}"])
                            tr.op("DVE", lambda e, _hp=hp, _gp=gp: e.tensor_mul(
                                tmp2[_hp][:, :], psStat[:, :], st_s[_gp][:, :]),
                                r=["psStat", f"st{gp}"], w=[f"tmp2{hp}"])
                            tr.op("DVE", lambda e, _d=dst, _h=h, _gp=gp, _hp=hp: e.tensor_add(
                                _d[:, _h, _gp * 512:(_gp + 1) * 512],
                                tmp1[_hp][:, :], tmp2[_hp][:, :]),
                                r=[f"tmp1{hp}", f"tmp2{hp}"], w=[f"{dkey}.{h}.g{g}"])

            def scanwout(g, lcs=(0, 1, 2, 3), do_wout=True):
                gp = g % 2
                for lc in lcs:
                    cg = g * 4 + lc
                    cp = cg % 2
                    slot = cg % 8
                    csl = slice(lc * 128, (lc + 1) * 128)
                    rsl = slice(gp * 512 + lc * 128, gp * 512 + (lc + 1) * 128)
                    tr.op("ACT", lambda e, _gp=gp, _lc=lc: e.activation(
                        out=egml_s[:, :], in_=tmg_s[_gp][:, _lc * 16:_lc * 16 + 8],
                        func=AF.Exp),
                        r=[f"tmg{gp}", "uegate"], w=["egml"])
                    tr.op("DVE", lambda e, _gp=gp, _lc=lc: e.tensor_add(
                        ue8_s[:, :], tmg_s[_gp][:, _lc * 16:_lc * 16 + 8],
                        tmg_s[_gp][:, _lc * 16 + 8:_lc * 16 + 16]),
                        r=[f"tmg{gp}", "uegate"], w=["ue8"])
                    tr.op("ACT", lambda e: e.activation(out=ue8_s[:, :], in_=ue8_s[:, :],
                                                        func=AF.Exp),
                          r=["ue8"], w=["ue8"])
                    tr.op("ACT", lambda e, _gp=gp, _lc=lc: e.activation(
                        out=ac8_s[:, :], in_=tmg_s[_gp][:, _lc * 16 + 8:_lc * 16 + 16],
                        func=AF.Exp),
                        r=[f"tmg{gp}"], w=["ac8"])
                    tr.op("DVE", lambda e, _gp=gp, _csl=csl: e.tensor_mul(
                        em8_s[:, :].rearrange("p (h t) -> p h t", h=NH),
                        ehat_s[_gp][:, _csl].rearrange("p (o t) -> p o t", o=1)
                        .broadcast_to([8, NH, 128]),
                        hmask_s[:, :].rearrange("p (h t) -> p h t", h=NH)),
                        r=[f"ehat{gp}", "c:hmask"], w=["em8"])

                    def _bc(e):
                        for bk in range(2):
                            i = e.matmul(psE[:, bk * 512:(bk + 1) * 512],
                                         lhsT=onesb_s[0:8, :],
                                         rhs=em8_s[:, bk * 512:(bk + 1) * 512],
                                         start=True, stop=True)
                        return i
                    tr.op("PE", _bc, r=["em8", "c:onesb", "psE"], w=["psE"])
                    tr.op("DVE", lambda e, _cp=cp, _rsl=rsl: e.tensor_mul(
                        qtl[_cp][:, :, :],
                        qb[:, :, _rsl],
                        psE[:, :].rearrange("p (h t) -> p h t", h=NH)),
                        r=[f"qb.{h}.g{g}" for h in range(NH)] + ["psE"],
                        w=[f"qtl{cp}"])

                    def _ska(e, _cp=cp, _rsl=rsl):
                        for h in range(NH):
                            e.transpose(psXK[:, h * 128:(h + 1) * 128],
                                        kb[:, h, _rsl], eyeb_s[:, :])
                            i = e.matmul(psA[:, h * 128:(h + 1) * 128],
                                         lhsT=kb[:, h, _rsl], rhs=qtl[_cp][:, h, :],
                                         start=True, stop=True)
                        return i
                    tr.op("PE", _ska,
                          r=[f"kb.{h}.g{g}" for h in range(NH)]
                          + [f"qtl{cp}", "c:eyeb", "psXK", "psA"],
                          w=["psXK", "psA"])
                    tr.op("DVE", lambda e: e.tensor_mul(
                        pbt_s[:, :, :],
                        psA[:, :].rearrange("p (h t) -> p h t", h=NH),
                        egml_s[:, :].rearrange("p (h o) -> p h o", o=1)
                        .broadcast_to([128, NH, 128])),
                        r=["psA", "egml"], w=["pbt"])
                    tr.op("DVE", lambda e: e.tensor_mul(
                        pbt_s[:, :, :].rearrange("p h t -> p (h t)"),
                        pbt_s[:, :, :].rearrange("p h t -> p (h t)"),
                        mask8_s[:, :]),
                        r=["pbt", "c:mask8"], w=["pbt"])
                    tr.op("ACT", lambda e: e.activation(
                        out=ktm_s[:, :, :].rearrange("p h t -> p (h t)"), in_=psXK[:, :],
                        func=AF.Copy),
                        r=["psXK"], w=["ktm"])
                    if cg < NCH - 1:
                        def _vtg(e, _s=slot):
                            for h in range(NH):
                                i = e.tensor_scalar(
                                    out=vtl_s[:, h, :],
                                    in0=vb[:, _s, h * 128:(h + 1) * 128],
                                    scalar1=ue8_s[:, h:h + 1], scalar2=None, op0=ALU.mult)
                            return i
                        tr.op("DVE", _vtg, r=[f"vb.{slot}", "ue8"], w=["vtl", "uegate"])

                        def _kvg(e):
                            for h in range(NH):
                                i = e.matmul(psA[:, h * 128:(h + 1) * 128],
                                             lhsT=ktm_s[:, h, :], rhs=vtl_s[:, h, :],
                                             start=True, stop=True)
                            return i
                        tr.op("PE", _kvg, r=["ktm", "vtl", "pbt", "psA"],
                              w=["psA"])
                    if cg > 0:
                        def _yinter(e, _cp=cp):
                            for h in range(NH):
                                i = e.matmul(psE[:, h * 128:(h + 1) * 128],
                                             lhsT=mbb_s[:, h, :], rhs=qtl[_cp][:, h, :],
                                             start=True, stop=True)
                            return i
                        tr.op("PE", _yinter, r=["mbb", f"qtl{cp}", "psE"], w=["psE"])
                    if cg < NCH - 1:
                        if cg == 0:
                            for bk in range(2):
                                tr.op("DVE", lambda e, _b=bk: e.tensor_copy(
                                    out=msb_s[:, 4 * _b:4 * (_b + 1), :],
                                    in_=psA[:, _b * 512:(_b + 1) * 512]
                                    .rearrange("p (h t) -> p h t", h=4)),
                                    r=["psA"], w=["msb"])
                        else:
                            def _mdec(e):
                                for h in range(NH):
                                    i = e.tensor_scalar(
                                        out=msb_s[:, h, :], in0=msb_s[:, h, :],
                                        scalar1=ac8_s[:, h:h + 1], scalar2=None,
                                        op0=ALU.mult)
                                return i
                            tr.op("DVE", _mdec, r=["msb", "ac8"], w=["msb"])
                            for bk in range(2):
                                tr.op("DVE", lambda e, _b=bk: e.tensor_add(
                                    msb_s[:, 4 * _b:4 * (_b + 1), :],
                                    msb_s[:, 4 * _b:4 * (_b + 1), :],
                                    psA[:, _b * 512:(_b + 1) * 512]
                                    .rearrange("p (h t) -> p h t", h=4)),
                                    r=["msb", "psA"], w=["msb"])
                        tr.op("ACT", lambda e: e.activation(
                            out=mbb_s[:, :, :].rearrange("p h t -> p (h t)"),
                            in_=msb_s[:, :, :].rearrange("p h t -> p (h t)"), func=AF.Copy),
                            r=["msb"], w=["mbb"])
                    def _yintra(e, _s=slot):
                        for h in range(NH):
                            i = e.matmul(psA[:, h * 128:(h + 1) * 128],
                                         lhsT=vb[:, _s, h * 128:(h + 1) * 128],
                                         rhs=pbt_s[:, h, :],
                                         start=True, stop=True)
                        return i
                    tr.op("PE", _yintra, r=[f"vb.{slot}", "pbt", "psA"], w=["psA"])
                    if cg > 0:
                        for bk in range(2):
                            tr.op("DVE", lambda e, _lc=lc, _b=bk: e.tensor_copy(
                                out=yring[:, 4 * _b:4 * (_b + 1), _lc * 128:(_lc + 1) * 128],
                                in_=psE[:, _b * 512:(_b + 1) * 512]
                                .rearrange("p (h t) -> p h t", h=4)),
                                r=["psE"], w=[f"yring.{lc}"])
                        for bk in range(2):
                            tr.op("DVE", lambda e, _lc=lc, _b=bk: e.tensor_add(
                                yring[:, 4 * _b:4 * (_b + 1), _lc * 128:(_lc + 1) * 128],
                                yring[:, 4 * _b:4 * (_b + 1), _lc * 128:(_lc + 1) * 128],
                                psA[:, _b * 512:(_b + 1) * 512]
                                .rearrange("p (h t) -> p h t", h=4)),
                                r=["psA", f"yring.{lc}"], w=[f"yring.{lc}"])
                    else:
                        for bk in range(2):
                            tr.op("DVE", lambda e, _lc=lc, _b=bk: e.tensor_copy(
                                out=yring[:, 4 * _b:4 * (_b + 1), _lc * 128:(_lc + 1) * 128],
                                in_=psA[:, _b * 512:(_b + 1) * 512]
                                .rearrange("p (h t) -> p h t", h=4)),
                                r=["psA"], w=[f"yring.{lc}"])
                if not do_wout:
                    return
                for wh in range(2):
                    for dcp in range(DC):
                        pp, pk = (psP0, "psP0") if dcp % 2 == 0 else (psP1, "psP1")

                        def _wog(e, _dcp=dcp, _pp=pp, _wh=wh):
                            for h in range(NH):
                                i = e.matmul(
                                    _pp[:, 0:256],
                                    lhsT=woutb[:, h, _dcp * 128:(_dcp + 1) * 128],
                                    rhs=yring[:, h, _wh * 256:(_wh + 1) * 256],
                                    start=(h == 0), stop=(h == NH - 1))
                            return i
                        tr.op("PE", _wog,
                              r=["woutb"] + [f"yring.{i}" for i in range(4)] + [pk], w=[pk])
                        tr.op("DVE", lambda e, _dcp=dcp, _pp=pp: e.tensor_scalar(
                            out=aosl[:, _dcp, :], in0=_pp[:, 0:256],
                            scalar1=bout_s[:, _dcp:_dcp + 1], scalar2=None, op0=ALU.add),
                            r=[pk, "c:bout"], w=["aosl"])
                    tr.dma("POOLQ", "d_ao",
                           lambda e, _g=g, _wh=wh: e.dma_start(
                               out=x1h.rearrange("(c p) t -> p c t", p=128)
                               [:, :, _g * 512 + _wh * 256:_g * 512 + (_wh + 1) * 256],
                               in_=aosl[:, :, :], accum_op=ALU.add),
                           r=["aosl", f"x1h.{g}"], w=[f"x1h.{g}"])

            frontS(0)
            if upto < 2:
                raise _Done()
            for g in range(NG):
                proj(g)
                if upto < 3:
                    raise _Done()
                frontG(g)
                if g == 0:
                    tr.dma("ACTQ", "d_x1init", lambda e: e.dma_start(out=x1h, in_=xTf),
                           w=[f"x1h.{gg}" for gg in range(NG)])
                if g > 0:
                    scanwout(g - 1, (0, 1), False)
                if g + 1 < NG:
                    frontS(g + 1)
                if g > 0:
                    scanwout(g - 1, (2, 3), True)
            scanwout(NG - 1)

        tr.barrier()
        with ExitStack() as ph6:
            _e6 = ph6.enter_context
            wib = _e6(nc.sbuf_tensor("wibuf", [128, DC, FF], BF16))
            wob = _e6(nc.sbuf_tensor("wobuf", [128, FC, D], BF16))
            bcat = _e6(nc.sbuf_tensor("bcat", [32, FF], BF16))
            gab = _e6(nc.sbuf_tensor("gab", [128, DC, 8], BF16))
            aab = _e6(nc.sbuf_tensor("aab", [128, DC, 32], BF16))
            x1sl = _e6(nc.sbuf_tensor("x1sl", [128, DC, 512], BF16))
            fsc = _e6(nc.sbuf_tensor("fsc", [128, 512], BF16))
            xnb = [_e6(nc.sbuf_tensor(f"xnb{i}", [128, DC, 512], BF16)) for i in range(2)]
            inter = _e6(nc.sbuf_tensor("inter", [128, FC, 512], BF16))
            x2sl = [_e6(nc.sbuf_tensor("x2sl0", [128, 512], F32))] * 2
            sqscr = _e6(nc.sbuf_tensor("sqscr", [128, 512], BF16))
            m2bc = _e6(nc.sbuf_tensor("m2bc", [128, 512], F32))
            v2bc = _e6(nc.sbuf_tensor("v2bc", [128, 512], F32))
            rp8 = _e6(nc.sbuf_tensor("rp8", [8, 512], F32))
            pp32 = _e6(nc.sbuf_tensor("pp32", [32, 512], BF16))
            tm8 = _e6(nc.sbuf_tensor("tm8", [128, 32], F32))
            mx8 = _e6(nc.sbuf_tensor("mx8", [128, 8], F32))
            ix8 = _e6(nc.sbuf_tensor("ix8", [128, 8], U32))
            ixf = _e6(nc.sbuf_tensor("ixf", [128, 4], F32))
            m8t = _e6(nc.sbuf_tensor("m8t", [128, 32], F32))
            m8f = _e6(nc.sbuf_tensor("m8f", [8, 512], BF16))
            psS1 = _e6(nc.psum_tensor("psS1", [128, 512], F32))
            psS2 = _e6(nc.psum_tensor("psS2", [128, 512], F32))
            psR8 = _e6(nc.psum_tensor("psR8", [128, 512], F32))
            psP32 = _e6(nc.psum_tensor("psP32", [32, 512], F32))
            psF0 = _e6(nc.psum_tensor("psF0", [128, 512], F32))
            psF1 = _e6(nc.psum_tensor("psF1", [128, 512], F32))
            psO0 = _e6(nc.psum_tensor("psO0", [128, 512], F32))
            psO1 = _e6(nc.psum_tensor("psO1", [128, 512], F32))
            if upto < 5:
                raise _Done()
            tr.dma("POOLQ", "d_ga",
                   lambda e: e.dma_start(out=gab[:, :, :],
                                         in_=gw8T.rearrange("(c p) e -> p c e", p=128)),
                   w=["gab"])
            tr.dma("POOLQ", "d_aa",
                   lambda e: e.dma_start(out=aab[:, :, :],
                                         in_=acatT.rearrange("(c p) e -> p c e", p=128)),
                   w=["aab"])
            tr.dma("POOLQ", "d_bcat", lambda e: e.dma_start(out=bcat[:, :], in_=bcatT),
                   w=["bcat"])

            def load_ffn_weights():
                for i in range(4):
                    tr.dma("POOLQ", f"d_wi{i}",
                           lambda e, _i=i: e.dma_start(
                               out=wib[:, :, _i * 1024:(_i + 1) * 1024],
                               in_=wiT.rearrange("(c p) f -> p c f", p=128)
                               [:, :, _i * 1024:(_i + 1) * 1024]),
                           w=[f"wib.{i}"])
                for i in range(4):
                    tr.dma("POOLQ", f"d_wo{i}",
                           lambda e, _i=i: e.dma_start(
                               out=wob[:, 8 * _i:8 * _i + 8, :],
                               in_=woT.rearrange("(c p) f -> p c f", p=128)[:, 8 * _i:8 * _i + 8, :]),
                           w=[f"wob.{i}"])

            def frontA(g):
                bb = g % 2
                tr.dma("POOLQ", "d_x1sl",
                       lambda e, _g=g: e.dma_start(
                           out=x1sl[:, :, :],
                           in_=x1h.rearrange("(c p) t -> p c t", p=128)
                           [:, :, _g * 512:(_g + 1) * 512]),
                       r=[f"x1h.{g}"], w=["x1sl"])
                tr.dma("SYNCQ", f"d_oinit{g}",
                       lambda e, _g=g: e.dma_start(
                           out=outT[:, _g * 512:(_g + 1) * 512],
                           in_=x1h[:, _g * 512:(_g + 1) * 512]),
                       r=[f"x1h.{g}"], w=[f"outT.{g}"])

                def _s1g(e):
                    for dc in range(DC):
                        i = e.matmul(psS1[:, :], lhsT=onesb_s[:, :], rhs=x1sl[:, dc, :],
                                     start=(dc == 0), stop=(dc == DC - 1))
                    return i
                tr.op("PE", _s1g, r=["c:onesb", "x1sl", "psS1"], w=["psS1"])
                for dc in range(DC):
                    tr.op("ACT", lambda e, _dc=dc: e.activation(out=sqscr[:, :],
                                                                in_=x1sl[:, _dc, :],
                                                                func=AF.Square),
                          r=["x1sl"], w=["sqscr"])
                    tr.op("PE", lambda e, _dc=dc: e.matmul(
                        psS2[:, :], lhsT=onesb_s[:, :], rhs=sqscr[:, :],
                        start=(_dc == 0), stop=(_dc == DC - 1)),
                        r=["c:onesb", "sqscr", "psS2"], w=["psS2"])
                tr.op("ACT", lambda e: e.activation(out=m2bc[:, :], in_=psS1[:, :],
                                                    func=AF.Copy, scale=1.0 / D),
                      r=["psS1"], w=["m2bc"])
                tr.op("ACT", lambda e: e.activation(out=v2bc[:, :], in_=psS2[:, :],
                                                    func=AF.Copy, scale=1.0 / D),
                      r=["psS2"], w=["v2bc"])
                tr.op("DVE", lambda e: e.tensor_mul(fsc[:, :], m2bc[:, :], m2bc[:, :]),
                      r=["m2bc"], w=["fsc"])
                tr.op("DVE", lambda e: e.tensor_sub(v2bc[:, :], v2bc[:, :], fsc[:, :]),
                      r=["v2bc", "fsc"], w=["v2bc"])
                tr.op("ACT", lambda e: e.activation(out=v2bc[:, :], in_=v2bc[:, :],
                                                    func=AF.Ln, bias=cb_s[:, 2:3]),
                      r=["v2bc", "c:cb"], w=["v2bc"])
                tr.op("ACT", lambda e: e.activation(out=v2bc[:, :], in_=v2bc[:, :],
                                                    func=AF.Exp, scale=-0.5),
                      r=["v2bc"], w=["v2bc"])
                for dc in range(DC):
                    tr.op("DVE", lambda e, _dc=dc: e.tensor_sub(fsc[:, :], x1sl[:, _dc, :],
                                                                m2bc[:, :]),
                          r=["x1sl", "m2bc"], w=["fsc"])
                    tr.op("DVE", lambda e, _dc=dc, _bb=bb: e.tensor_mul(
                        xnb[_bb][:, _dc, :], fsc[:, :], v2bc[:, :]),
                        r=["fsc", "v2bc"], w=[f"xnb{bb}"])

            def frontB(g):
                bb = g % 2

                def _rg(e, _bb=bb):
                    for dc in range(DC):
                        e.matmul(psR8[0:8, :], lhsT=gab[:, dc, :], rhs=xnb[_bb][:, dc, :],
                                 start=(dc == 0), stop=(dc == DC - 1))
                        i = e.matmul(psP32[:, :], lhsT=aab[:, dc, :], rhs=xnb[_bb][:, dc, :],
                                     start=(dc == 0), stop=(dc == DC - 1))
                    return i
                tr.op("PE", _rg, r=["gab", "aab", f"xnb{bb}", "psR8", "psP32"],
                      w=["psR8", "psP32"])
                tr.op("DVE", lambda e: e.tensor_scalar(out=rp8[:, :], in0=psR8[0:8, :],
                                                       scalar1=g8_s[:, :], scalar2=None,
                                                       op0=ALU.add),
                      r=["psR8", "c:g8"], w=["rp8"])
                tr.op("ACT", lambda e: e.activation(out=pp32[:, :], in_=psP32[:, :],
                                                    func=AF.Copy),
                      r=["psP32"], w=["pp32"])
                for tc4 in range(4):
                    tr.op("PE", lambda e, _tc=tc4: e.transpose(
                        psS1[:, _tc * 8:(_tc + 1) * 8], rp8[:, _tc * 128:(_tc + 1) * 128],
                        eyef_s[0:8, 0:8]),
                        r=["rp8", "c:eyef", "psS1"], w=["psS1"])
                pe_settle(psS1[32:34, 40:42], ["psS1", "rp8"])
                tr.op("ACT", lambda e: e.activation(
                    out=tm8[:, :], in_=psS1[:, 0:32], func=AF.Copy),
                    r=["psS1"], w=["tm8"])
                for tc4 in range(4):
                    tr.op("DVE", lambda e, _tc=tc4: e.max(
                        out=mx8[:, :], in_=tm8[:, _tc * 8:(_tc + 1) * 8]),
                        r=["tm8"], w=["mx8"])
                    tr.op("DVE", lambda e, _tc=tc4: e.max_index(
                        out=ix8[:, :], in_max=mx8[:, :],
                        in_values=tm8[:, _tc * 8:(_tc + 1) * 8]),
                        r=["mx8", "tm8"], w=["ix8"])
                    tr.op("DVE", lambda e, _tc=tc4: e.tensor_copy(
                        out=ixf[:, _tc:_tc + 1], in_=ix8[:, 0:1]),
                        r=["ix8"], w=["ixf"])
                    tr.op("DVE", lambda e, _tc=tc4: e.tensor_scalar(
                        out=m8t[:, _tc * 8:(_tc + 1) * 8], in0=iota8_s[:, :],
                        scalar1=ixf[:, _tc:_tc + 1], scalar2=None, op0=ALU.is_equal),
                        r=["c:iota", "ixf"], w=["m8t"])
                for tc4 in range(4):
                    tr.op("PE", lambda e, _tc=tc4: e.transpose(
                        psR8[0:8, _tc * 128:(_tc + 1) * 128], m8t[:, _tc * 8:(_tc + 1) * 8],
                        eyef_s[:, :]),
                        r=["m8t", "c:eyef", "psR8"], w=["psR8"])
                pe_settle(psR8[32:34, 40:42], ["psR8", "m8t"])
                tr.op("ACT", lambda e: e.activation(
                    out=m8f[:, :], in_=psR8[0:8, :], func=AF.Copy),
                    r=["psR8"], w=["m8f"])
                tr.op("PE", lambda e: e.matmul(psP32[:, :], lhsT=expand_s[:, :], rhs=m8f[:, :],
                                               start=True, stop=True),
                      r=["c:exp", "m8f", "psP32"], w=["psP32"])
                tr.op("DVE", lambda e: e.tensor_mul(pp32[:, :], pp32[:, :], psP32[:, :]),
                      r=["pp32", "psP32"], w=["pp32"])

            def body_wi(g):
                bb = g % 2
                for fc in range(FC):
                    pp, pk = (psF0, "psF0") if fc % 2 == 0 else (psF1, "psF1")

                    def _wig(e, _fc=fc, _pp=pp, _bb=bb):
                        for dc in range(DC):
                            e.matmul(_pp[:, :],
                                     lhsT=wib[:, dc, _fc * 128:(_fc + 1) * 128],
                                     rhs=xnb[_bb][:, dc, :], start=(dc == 0), stop=False)
                        return e.matmul(_pp[:, :],
                                        lhsT=bcat[:, _fc * 128:(_fc + 1) * 128],
                                        rhs=pp32[:, :], start=False, stop=True)
                    tr.op("PE", _wig,
                          r=[f"wib.{fc // 8}", f"xnb{bb}", "bcat", "pp32", pk],
                          w=[pk])
                    tr.op("ACT", lambda e, _fc=fc, _pp=pp: e.activation(
                        out=inter[:, _fc, :], in_=_pp[:, :], func=AF.Relu,
                        bias=wib_s[:, _fc:_fc + 1]),
                        r=[pk, "c:wib"], w=[f"inter.{fc}"])

            def body_wo(g, d0, d1):
                for dcp in range(d0, d1):
                    pp, pk = (psO0, "psO0") if dcp % 2 == 0 else (psO1, "psO1")

                    def _wog2(e, _dcp=dcp, _pp=pp):
                        for fc in range(FC):
                            i = e.matmul(_pp[:, :],
                                         lhsT=wob[:, fc, _dcp * 128:(_dcp + 1) * 128],
                                         rhs=inter[:, fc, :], start=(fc == 0),
                                         stop=(fc == FC - 1))
                        return i
                    tr.op("PE", _wog2,
                          r=[f"wob.{i}" for i in range(4)]
                          + [f"inter.{fc}" for fc in range(FC)] + [pk],
                          w=[pk])
                    tr.op("DVE", lambda e, _dcp=dcp, _pp=pp: e.tensor_scalar(
                        out=x2sl[0][:, :], in0=_pp[:, :], scalar1=wob_s[:, _dcp:_dcp + 1],
                        scalar2=None, op0=ALU.add),
                        r=[pk, "c:wob", "x2sl0"], w=["x2sl0"])
                    tr.dma("POOLQ", "d_out0",
                           lambda e, _g=g, _dcp=dcp: e.dma_start(
                               out=outT[_dcp * 128:(_dcp + 1) * 128,
                                        _g * 512:(_g + 1) * 512],
                               in_=x2sl[0][:, :], accum_op=ALU.add),
                           r=["x2sl0", f"outT.{g}"], w=[f"outT.{g}"])

            frontA(0)
            frontB(0)
            load_ffn_weights()
            for g in range(NG):
                body_wi(g)
                if g + 1 < NG:
                    frontA(g + 1)
                body_wo(g, 0, 4)
                if g + 1 < NG:
                    frontB(g + 1)
                body_wo(g, 4, 8)
        raise _Done()
    except _Done:
        with ExitStack() as c2:
            tr.final_wait("POOLQ", "d_out0")
            tr.build_blocks(nc, c2)
    return nc


_NC_CACHE = {}


def _get_nc():
    if "nc" not in _NC_CACHE:
        _NC_CACHE["nc"] = build()
    return _NC_CACHE["nc"]


def kernel(**inputs):
    x = np.asarray(inputs["L0_attn_x"])
    nc = _get_nc()
    shared = _shared_arrays(inputs)
    in_maps = []
    for c in range(8):
        m = dict(shared)
        xt = np.ascontiguousarray(x[c].T)
        m["xT"] = xt
        m["xTf"] = xt
        in_maps.append(m)
    res = run_bass_kernel_spmd(nc, in_maps, core_ids=list(range(8)))
    return np.stack([np.asarray(r["outT"]).T for r in res.results], axis=0)


def _shared_arrays(inputs):
    ii = {k: np.asarray(v) for k, v in inputs.items()}
    wgP = np.zeros((D, 128), np.float32)
    wgP[:, 0:8] = ii["L0_attn_Wbeta"].T
    wgP[:, 32:40] = ii["L0_attn_Wig"].T
    wgP[:, 64:72] = ii["L0_attn_Wog"].T
    b128 = np.zeros((128, 1), np.float32)
    b128[0:8, 0] = ii["L0_attn_bbeta"]
    b128[32:40, 0] = ii["L0_attn_big"]
    b128[64:72, 0] = ii["L0_attn_bog"]
    return {
        "wqT": np.ascontiguousarray(ii["L0_attn_Wq"].T),
        "wkT": np.ascontiguousarray(ii["L0_attn_Wk"].T),
        "wvT": np.ascontiguousarray(ii["L0_attn_Wv"].T),
        "woutT": np.ascontiguousarray(ii["L0_attn_Wout"].T),
        "wgP": wgP, "b128": b128,
        "nw8": np.ascontiguousarray(ii["L0_attn_norm_w"].reshape(DC, 128).T),
        "gw8T": np.ascontiguousarray(ii["L0_ffn_gate_W"].T),
        "acatT": np.ascontiguousarray(
            np.transpose(ii["L0_ffn_lora_A"], (2, 0, 1)).reshape(D, 32)),
        "g8": np.ascontiguousarray(ii["L0_ffn_gate_b"].reshape(8, 1)),
        "bcatT": np.ascontiguousarray(
            np.transpose(ii["L0_ffn_lora_B"], (0, 2, 1)).reshape(32, FF)),
        "wiT": np.ascontiguousarray(ii["L0_ffn_wi_W"].T),
        "woT": np.ascontiguousarray(ii["L0_ffn_wo_W"].T),
        "wib32": np.ascontiguousarray(ii["L0_ffn_wi_b"].reshape(FC, 128).T),
        "wob8": np.ascontiguousarray(ii["L0_ffn_wo_b"].reshape(DC, 128).T),
        "bout8": np.ascontiguousarray(ii["L0_attn_bout"].reshape(DC, 128).T),
    }
